# version 12
# baseline (speedup 1.0000x reference)
import math
import contextlib
import numpy as np
import ml_dtypes
import concourse.bass as bass
from concourse.aot_env import allow_ici_all_to_all
import concourse.mybir as mybir
from concourse.bass_utils import run_bass_kernel_spmd

F32 = mybir.dt.float32
BF16 = mybir.dt.bfloat16
I32 = mybir.dt.int32
AF = mybir.ActivationFunctionType
ALU = mybir.AluOpType
AX = mybir.AxisListType

D = 4096
NCORE, HPC = 8, 4
TOK, NT = 1024, 8192
CH = 32
NTC = 16
NU, NQC = 8, 8
EPS = 1e-5
NEGBIG = -1.0e13
LAG = 3

PAIRS = []
for _qc in range(NQC):
    _nk = 4 * (_qc + 1)
    for _kt in range(_nk):
        PAIRS.append((_qc, _kt, _nk, _kt == 0, _kt == _nk - 1))
NPAIR = len(PAIRS)


def _alibi_slopes(n_heads):
    cp2 = 2 ** math.floor(math.log2(n_heads))
    base = 2 ** (-(2 ** (-(math.log2(cp2) - 3))))
    slopes = base ** np.arange(1, cp2 + 1, dtype=np.float32)
    if cp2 != n_heads:
        extra_base = 2 ** (-(2 ** (-(math.log2(2 * cp2) - 3))))
        n_rem = min(cp2, n_heads - cp2)
        extra = extra_base ** np.arange(1, 1 + 2 * n_rem, 2, dtype=np.float32)
        slopes = np.concatenate([slopes, extra])
    return slopes.astype(np.float32)


def build_program():
    nc = bass.Bass("TRN2", target_bir_lowering=False)

    x_own = nc.dram_tensor("x_own", [TOK, D], F32, kind="ExternalInput").ap()
    wqk = nc.dram_tensor("wqk", [D, 1024], F32, kind="ExternalInput").ap()
    wv_in = nc.dram_tensor("wv_in", [D, 512], F32, kind="ExternalInput").ap()
    bqk_in = nc.dram_tensor("bqk_in", [128, 8], F32, kind="ExternalInput").ap()
    bv_in = nc.dram_tensor("bv_in", [1, 512], F32, kind="ExternalInput").ap()
    w_proj = nc.dram_tensor("w_proj", [D, D], F32, kind="ExternalInput").ap()
    b_proj = nc.dram_tensor("b_proj", [1, D], F32, kind="ExternalInput").ap()
    base_WT = nc.dram_tensor("base_WT", [D, D], F32, kind="ExternalInput").ap()
    base_b = nc.dram_tensor("base_b", [1, D], F32, kind="ExternalInput").ap()
    router_W = nc.dram_tensor("router_W", [15, D], F32, kind="ExternalInput").ap()
    router_b = nc.dram_tensor("router_b", [15, 1], F32, kind="ExternalInput").ap()
    A_in = nc.dram_tensor("A_in", [128, D], F32, kind="ExternalInput").ap()
    B_in = nc.dram_tensor("B_in", [128, D], F32, kind="ExternalInput").ap()
    emap = nc.dram_tensor("emap", [15, 1], I32, kind="ExternalInput").ap()
    et_in = nc.dram_tensor("et_in", [128, 20, 512], BF16, kind="ExternalInput").ap()
    slope_in = nc.dram_tensor("slope_in", [128, HPC], F32, kind="ExternalInput").ap()
    btab_in = nc.dram_tensor("btab_in", [128, HPC * 32], F32, kind="ExternalInput").ap()
    idb_in = nc.dram_tensor("idb_in", [128, 128], BF16, kind="ExternalInput").ap()
    idf_in = nc.dram_tensor("idf_in", [128, 128], F32, kind="ExternalInput").ap()
    er_in = nc.dram_tensor("er_in", [15, 8], F32, kind="ExternalInput").ap()
    ex_in = nc.dram_tensor("ex_in", [8, 128], F32, kind="ExternalInput").ap()
    out_own = nc.dram_tensor("out_own", [TOK, D], F32, kind="ExternalOutput").ap()

    NJ = 2
    agi = nc.dram_tensor("agi", [NJ * D, 512], BF16).ap()
    ago = nc.dram_tensor("ago", [NJ * NCORE * D, 512], BF16).ap()
    qT_loc = nc.dram_tensor("qT_loc", [HPC, 128, NT], BF16).ap()
    kT_loc = nc.dram_tensor("kT_loc", [HPC, 128, NT], BF16).ap()
    v_loc = nc.dram_tensor("v_loc", [NT // 128, 128, 512], BF16).ap()
    a2a_in = nc.dram_tensor("a2a_in", [NCORE * 512, TOK], BF16).ap()
    a2a_out = nc.dram_tensor("a2a_out", [NCORE * 512, TOK], BF16).ap()
    acc_d = nc.dram_tensor("acc_d", [8, 128, 8, 512], BF16).ap()
    wpb = nc.dram_tensor("wpb", [D, D], BF16).ap()
    bwb = nc.dram_tensor("bwb", [D, D], BF16).ap()

    agi_v = agi.rearrange("(j c p) q -> j c p q", j=NJ, p=128)
    agi_x = agi.rearrange("(j c p) q -> p c j q", j=NJ, p=128)
    ago_v = ago.rearrange("(j s c p) q -> j s c p q", j=NJ, s=NCORE, p=128)
    a2a_in_v = a2a_in.rearrange("(s c) q -> s c q", s=NCORE)
    a2a_out_v = a2a_out.rearrange("(s c) q -> s c q", s=NCORE)
    groups = [list(range(NCORE))]

    with contextlib.ExitStack() as _st:
        ec = _st.enter_context
        c_slope = ec(nc.sbuf_tensor("c_slope", [128, HPC], F32))
        c_bias = ec(nc.sbuf_tensor("c_bias", [128, HPC * 32], F32))
        c_idb = ec(nc.sbuf_tensor("c_idb", [128, 128], BF16))
        c_idf = ec(nc.sbuf_tensor("c_idf", [128, 128], F32))
        c_erange = ec(nc.sbuf_tensor("c_erange", [15, 8], F32))
        c_expand = ec(nc.sbuf_tensor("c_expand", [8, 128], F32))
        ones_bf_r = ec(nc.sbuf_tensor("ones_bf_r", [1, 128], BF16))
        ones_bf_sq = ec(nc.sbuf_tensor("ones_bf_sq", [128, 128], BF16))
        eps_sb = ec(nc.sbuf_tensor("eps_sb", [128, 1], F32))
        g_sb = ec(nc.sbuf_tensor("g_sb", [128, TOK], BF16))
        bp_sb = ec(nc.sbuf_tensor("bp_sb", [128, D], BF16))
        bias_bf = ec(nc.sbuf_tensor("bias_bf", [1, D], BF16))
        bv_bf = ec(nc.sbuf_tensor("bv_bf", [1, 512], BF16))
        sG = ec(nc.semaphore("sG"))
        sG2 = ec(nc.semaphore("sG2"))
        sT = ec(nc.semaphore("sT"))
        sV = ec(nc.semaphore("sV"))
        sC = ec(nc.semaphore("sC"))
        sCC = ec(nc.semaphore("sCC"))
        sCON = ec(nc.semaphore("sCON"))
        sCONA = ec(nc.semaphore("sCONA"))
        sSTG = [ec(nc.semaphore(f"sSTG{i}")) for i in range(2)]
        sAO = ec(nc.semaphore("sAO"))
        sX = [ec(nc.semaphore(f"sX{i}")) for i in range(2)]
        sXL = [ec(nc.semaphore(f"sXL{i}")) for i in range(3)]
        sX1 = [ec(nc.semaphore(f"sX1{i}")) for i in range(2)]
        sQK = [ec(nc.semaphore(f"sQK{i}")) for i in range(2)]
        sXO = [ec(nc.semaphore(f"sXO{i}")) for i in range(4)]
        sU = [ec(nc.semaphore(f"sU{i}")) for i in range(3)]
        sA = [ec(nc.semaphore(f"sA{i}")) for i in range(2)]
        sW0 = ec(nc.semaphore("sW0"))
        sW1 = ec(nc.semaphore("sW1"))
        sXO3 = [ec(nc.semaphore(f"sXO3{i}")) for i in range(4)]
        sAC = [ec(nc.semaphore(f"sAC{i}")) for i in range(2)]
        sWCl = [ec(nc.semaphore(f"sWCl{i}")) for i in range(3)]
        sW1p = [ec(nc.semaphore(f"sW1p{i}")) for i in range(2)]
        sWCs = [ec(nc.semaphore(f"sWCs{i}")) for i in range(3)]
        sACL = [ec(nc.semaphore(f"sACL{i}")) for i in range(2)]
        sX3 = [ec(nc.semaphore(f"sX3{i}")) for i in range(2)]
        sOUT = [ec(nc.semaphore(f"sOUT{i}")) for i in range(2)]
        sONE = ec(nc.semaphore("sONE"))
        block = ec(nc.Block())

        G_bw = 16 * 8
        G_p15 = 96
        CONA = 16 * 4
        CON0 = 16 * 3
        CON1 = CON0 + 16

        V_bn, V_ag, V_nmr, V_cp = {}, {}, {}, {}
        _vr = 0
        for tt in range(8):
            _vr += 1; V_bn[tt] = _vr
            _vr += 1; V_ag[tt] = _vr
            _vr += 1; V_nmr[tt] = _vr
            if tt >= 1:
                for g in range(4):
                    _vr += 1; V_cp[(tt - 1, g)] = _vr
        for g in range(4):
            _vr += 1; V_cp[(7, g)] = _vr
        V_p0 = _vr
        C_rstd = {tt: 2 * tt + 1 for tt in range(8)}
        C_norm = {tt: 2 * tt + 2 for tt in range(8)}
        C_p0 = 16
        T_tr = {(tt, g): 4 * tt + g + 1 for tt in range(8) for g in range(4)}
        T_p0 = 32

        T_aT = {dc: T_p0 + dc + 1 for dc in range(CH)}
        T_rwT = {dc: T_p0 + 32 + dc + 1 for dc in range(CH)}
        T_15dc = {dc: T_p0 + 64 + dc + 1 for dc in range(CH)}
        T_lgT = {tt: T_p0 + 96 + tt + 1 for tt in range(8)}
        T_mpT = {tt: T_p0 + 104 + tt + 1 for tt in range(8)}
        T_wmm = {t2: T_p0 + 112 + t2 + 1 for t2 in range(2)}
        T_wbmm = {t2: T_p0 + 114 + t2 + 1 for t2 in range(2)}
        T_p15 = T_p0 + 116
        V_aTe = {dc: V_p0 + dc + 1 for dc in range(CH)}
        V_rwTe = {dc: V_p0 + 32 + dc + 1 for dc in range(CH)}
        V_mmt = V_p0 + 65
        V_mmap = V_p0 + 66
        V_bsum = V_p0 + 67
        V_lgev = {t2: V_p0 + 67 + t2 + 1 for t2 in range(2)}
        V_lgTe = {tt: V_p0 + 69 + tt + 1 for tt in range(8)}
        V_smn = {tt: V_p0 + 77 + 2 * tt + 1 for tt in range(8)}
        V_sm = {tt: V_p0 + 77 + 2 * tt + 2 for tt in range(8)}
        V_mpTe = {tt: V_p0 + 93 + tt + 1 for tt in range(8)}
        V_wev = {t2: V_p0 + 101 + t2 + 1 for t2 in range(2)}
        V_wbev = {t2: V_p0 + 103 + t2 + 1 for t2 in range(2)}
        V_g = {t2: V_p0 + 105 + t2 + 1 for t2 in range(2)}
        V_p15 = V_p0 + 107
        C_mmap = C_p0 + 1
        C_smexp = {tt: C_p0 + 1 + tt + 1 for tt in range(8)}
        C_p15 = C_p0 + 9

        T_p1 = {(tc, k): T_p15 + 12 * tc + k + 1 for tc in range(NTC) for k in range(12)}
        V_p1 = {(tc, k): V_p15 + 12 * tc + k + 1 for tc in range(NTC) for k in range(12)}
        T_p1b = T_p15 + 12 * NTC
        V_p1b = V_p15 + 12 * NTC
        QK_p1b = 16 * 96

        T_S, T_denf, T_pvf = {}, {}, {}
        t_run = T_p1b
        for u in range(NU):
            for i in range(NPAIR + LAG):
                if i < NPAIR:
                    t_run += 1
                    T_S[u * NPAIR + i] = t_run
                j = i - LAG
                if j >= 0 and PAIRS[j][4]:
                    g = u * NQC + PAIRS[j][0]
                    t_run += 1; T_denf[g] = t_run
                    t_run += 1; T_pvf[g] = t_run
        T_p2 = t_run
        V_stt, V_rcp, V_onrm = {}, {}, {}
        v_run = V_p1b
        for u in range(NU):
            _pend = []
            for i in range(NPAIR + LAG + 2):
                if i < NPAIR:
                    v_run += 1
                    V_stt[u * NPAIR + i] = v_run
                j = i - LAG
                if 0 <= j < NPAIR and PAIRS[j][4]:
                    _pend.append((u * NQC + PAIRS[j][0], i + 2))
                while _pend and _pend[0][1] <= i:
                    g = _pend.pop(0)[0]
                    v_run += 1; V_rcp[g] = v_run
                    v_run += 1; V_onrm[g] = v_run
        V_p2 = v_run
        C_exp = {u * NPAIR + i: C_p15 + u * NPAIR + i + 1
                 for u in range(NU) for i in range(NPAIR)}
        C_p2 = C_p15 + NU * NPAIR
        U_val = {u: 96 * (u // 3 + 1) for u in range(NU)}

        T_dc3, T_a3, T_cc3, T_b3 = {}, {}, {}, {}
        t_run = T_p2
        for oc in range(8):
            for dc in range(CH):
                t_run += 1
                T_dc3[oc * CH + dc] = t_run
            for t8 in range(8):
                t_run += 1
                T_a3[(oc, t8)] = t_run
        for oc in range(8):
            for cc in range(CH - 1):
                t_run += 1
                T_cc3[oc * CH + cc] = t_run
            for t8 in range(8):
                t_run += 1
                T_b3[(oc, t8)] = t_run
        T_p3e = t_run
        V_a = {(oc, t8): V_p2 + 8 * oc + t8 + 1 for oc in range(8) for t8 in range(8)}
        V_ae = V_p2 + 64
        V_b = {(oc, t8): V_ae + 8 * oc + t8 + 1 for oc in range(8) for t8 in range(8)}
        V_p3e = V_ae + 64

        with contextlib.ExitStack() as _st2:
            wqk_sb = _st2.enter_context(nc.sbuf_tensor("wqk_sb", [128, CH, 1024], BF16))
            wv_sb = _st2.enter_context(nc.sbuf_tensor("wv_sb", [128, CH, 512], BF16))
            bqk_sb = _st2.enter_context(nc.sbuf_tensor("bqk_sb", [128, 8], F32))
            @block.gpsimd
            def _(gpsimd):
                gpsimd.dma_start(out=bp_sb[:, :], in_=B_in).then_inc(sG, 16)
                gpsimd.dma_start(out=bv_bf[:, :], in_=bv_in).then_inc(sG, 16)
                for i in range(4):
                    gpsimd.dma_start(
                        out=wqk_sb[:, i * 8:(i + 1) * 8, :],
                        in_=wqk.rearrange("(c p) n -> p c n", p=128)[:, i * 8:(i + 1) * 8, :],
                    ).then_inc(sG, 16)
                for i in range(2):
                    gpsimd.dma_start(
                        out=wv_sb[:, i * 16:(i + 1) * 16, :],
                        in_=wv_in.rearrange("(c p) n -> p c n", p=128)[:, i * 16:(i + 1) * 16, :],
                    ).then_inc(sG, 16)
                gpsimd.memset(ones_bf_r[:, :], 1.0)
                gpsimd.memset(ones_bf_sq[:, :], 1.0)
                gpsimd.memset(eps_sb[:, :], EPS).then_inc(sONE, 1)

            @block.sync
            def _(sync):
                for ap_d, ap_s in [
                    (c_slope[:, :], slope_in), (c_bias[:, :], btab_in),
                    (c_idb[:, :], idb_in), (c_idf[:, :], idf_in),
                ]:
                    sync.dma_start(out=ap_d, in_=ap_s).then_inc(sCONA, 16)
                for ap_d, ap_s in [
                    (c_erange[:, :], er_in), (c_expand[:, :], ex_in),
                    (bqk_sb[:, :], bqk_in),
                ]:
                    sync.dma_start(out=ap_d, in_=ap_s).then_inc(sCON, 16)

            with contextlib.ExitStack() as _st2:
                x_t = _st2.enter_context(nc.sbuf_tensor("x_t", [128, 3, D], F32))
                xn_t = _st2.enter_context(nc.sbuf_tensor("xn_t", [128, 2, D], BF16))
                stg = _st2.enter_context(nc.sbuf_tensor("stg", [128, 2, CH, 128], BF16))
                st6 = _st2.enter_context(nc.sbuf_tensor("st6", [128, 2, 48], F32))
                mv_sb = _st2.enter_context(nc.sbuf_tensor("mv_sb", [128, 2, 2], F32))
                r_sb = _st2.enter_context(nc.sbuf_tensor("r_sb", [128, 2], F32))
                sq_sb = _st2.enter_context(nc.sbuf_tensor("sq_sb", [128, 2], F32))
                nmr_sb = _st2.enter_context(nc.sbuf_tensor("nmr_sb", [128, 2], F32))
                p0_trA = _st2.enter_context(nc.psum_tensor("p0_trA", [128, 8, 128], BF16))
                p0_trB = _st2.enter_context(nc.psum_tensor("p0_trB", [128, 8, 128], BF16))
                p0_tr = [p0_trA, p0_trB]

                @block.sync
                def _(sync):
                    for tt in range(8):
                        px = tt % 3
                        if tt >= 3:
                            sync.wait_ge(sV, V_bn[tt - 3])
                            sync.wait_ge(sC, C_norm[tt - 3])
                        sync.dma_start(
                            out=x_t[:, px, :], in_=x_own[tt * 128:(tt + 1) * 128, :]
                        ).then_inc(sXL[px], 16)

                @block.gpsimd
                def _(gpsimd):
                    for j in range(NJ):
                        for tt in range(4 * j, 4 * j + 4):
                            gpsimd.wait_ge(sV, V_cp[(tt, 3)])
                            gpsimd.dma_start(
                                out=agi_v[tt // 4, :, :, (tt % 4) * 128:(tt % 4) * 128 + 128]
                                .rearrange("c p q -> p c q"),
                                in_=stg[:, tt % 2, :, :],
                            ).then_inc(sSTG[tt % 2], 16)
                        gpsimd.wait_ge(sSTG[0], 32 * (j + 1))
                        gpsimd.wait_ge(sSTG[1], 32 * (j + 1))
                        gpsimd.collective_compute(
                            "AllGather", ALU.bypass, replica_groups=groups,
                            ins=[agi[j * D:(j + 1) * D, :]],
                            outs=[ago[j * NCORE * D:(j + 1) * NCORE * D, :]],
                        ).then_inc(sCC, 1)

                def _p0_copies(vector, ct):
                    cp = ct % 2
                    for g in range(4):
                        vector.wait_ge(sT, T_tr[(ct, g)])
                        if ct >= 2 and g == 0:
                            vector.wait_ge(sSTG[cp], 16 * (ct // 2))
                        vector.tensor_copy(
                            stg[:, cp, g * 8:(g + 1) * 8, :],
                            p0_tr[(4 * ct + g) % 2][:, :, :]
                        ).then_inc(sV, 1)

                @block.vector
                def _(vector):
                    for tt in range(8):
                        p = tt % 2
                        px = tt % 3
                        vector.wait_ge(sXL[px], 16 * (tt // 3 + 1))
                        for sl in range(8):
                            bs = vector.bn_stats(
                                st6[:, p, 6 * sl:6 * sl + 6],
                                x_t[:, px, 512 * sl:512 * (sl + 1)])
                            if sl == 7:
                                bs.then_inc(sV, 1)
                        vector.drain()
                        if tt >= 2:
                            vector.wait_ge(sC, C_rstd[tt - 2])
                        vector.bn_aggr(mv_sb[:, p, :], st6[:, p, :]).then_inc(sV, 1)
                        vector.wait_ge(sC, C_rstd[tt])
                        if tt >= 2:
                            vector.wait_ge(sC, C_norm[tt - 2])
                        vector.reciprocal(r_sb[:, p:p + 1], sq_sb[:, p:p + 1])
                        vector.drain()
                        vector.scalar_tensor_tensor(
                            nmr_sb[:, p:p + 1], mv_sb[:, p, 0:1], -1.0, r_sb[:, p:p + 1],
                            op0=ALU.mult, op1=ALU.mult).then_inc(sV, 1)
                        if tt >= 1:
                            _p0_copies(vector, tt - 1)
                    _p0_copies(vector, 7)

                @block.scalar
                def _(scalar):
                    scalar.wait_ge(sONE, 1)
                    scalar.wait_ge(sCONA, CONA)
                    for tt in range(8):
                        p = tt % 2
                        scalar.wait_ge(sV, V_ag[tt])
                        scalar.activation(
                            sq_sb[:, p:p + 1], mv_sb[:, p, 1:2], AF.Sqrt,
                            bias=eps_sb[:, :], scale=1.0).then_inc(sC, 1)
                        scalar.wait_ge(sV, V_nmr[tt])
                        if tt >= 2:
                            scalar.wait_ge(sT, T_tr[(tt - 2, 3)])
                        scalar.activation(
                            xn_t[:, p, :], x_t[:, tt % 3, :], AF.Identity,
                            bias=nmr_sb[:, p:p + 1], scale=r_sb[:, p:p + 1]
                        ).then_inc(sC, 1)

                @block.tensor
                def _(tensor):
                    tensor.wait_ge(sCONA, CONA)
                    for tt in range(8):
                        p = tt % 2
                        tensor.wait_ge(sC, C_norm[tt])
                        for g in range(4):
                            gg = 4 * tt + g
                            if gg >= 2:
                                ptt, pg = divmod(gg - 2, 4)
                                tensor.wait_ge(sV, V_cp[(ptt, pg)])
                            for k in range(8):
                                dc = g * 8 + k
                                mm = tensor.transpose(
                                    p0_tr[gg % 2][:, k, :],
                                    xn_t[:, p, dc * 128:(dc + 1) * 128],
                                    c_idb[:, :])
                                if k == 7:
                                    mm.then_inc(sT, 1)

            with contextlib.ExitStack() as _st2:
                a_raw = _st2.enter_context(nc.sbuf_tensor("a_raw", [128, D], BF16))
                rw_raw = _st2.enter_context(nc.sbuf_tensor("rw_raw", [15, D], BF16))
                aT_sb = _st2.enter_context(nc.sbuf_tensor("aT_sb", [128, CH, 128], BF16))
                rwT_sb = _st2.enter_context(nc.sbuf_tensor("rwT_sb", [128, CH, 15], BF16))
                xo_sb = _st2.enter_context(nc.sbuf_tensor("xo_sb", [128, 4, 1024], BF16))
                em_i = _st2.enter_context(nc.sbuf_tensor("em_i", [15, 1], I32))
                em_f = _st2.enter_context(nc.sbuf_tensor("em_f", [15, 1], F32))
                mm_t = _st2.enter_context(nc.sbuf_tensor("mm_t", [15, 8], F32))
                mm_sq = _st2.enter_context(nc.sbuf_tensor("mm_sq", [15, 8], F32))
                mm_sq2 = _st2.enter_context(nc.sbuf_tensor("mm_sq2", [15, 8], F32))
                mmap = _st2.enter_context(nc.sbuf_tensor("mmap", [15, 8], F32))
                bp1 = _st2.enter_context(nc.sbuf_tensor("bp1", [1, D], BF16))
                bb1 = _st2.enter_context(nc.sbuf_tensor("bb1", [1, D], BF16))
                rb_sb = _st2.enter_context(nc.sbuf_tensor("rb_sb", [15, 1], F32))
                lg_sb = _st2.enter_context(nc.sbuf_tensor("lg_sb", [15, TOK], F32))
                lgT_sb = _st2.enter_context(nc.sbuf_tensor("lgT_sb", [128, 8, 15], F32))
                sm_nm = _st2.enter_context(nc.sbuf_tensor("sm_nm", [128, 1], F32))
                sm_e = _st2.enter_context(nc.sbuf_tensor("sm_e", [128, 15], F32))
                sm_s = _st2.enter_context(nc.sbuf_tensor("sm_s", [128, 1], F32))
                sm_r = _st2.enter_context(nc.sbuf_tensor("sm_r", [128, 1], F32))
                sm_p = _st2.enter_context(nc.sbuf_tensor("sm_p", [128, 15], F32))
                sm_m8 = _st2.enter_context(nc.sbuf_tensor("sm_m8", [128, 8], F32))
                mp_all = _st2.enter_context(nc.sbuf_tensor("mp_all", [128, 8, 15], F32))
                mpT_sb = _st2.enter_context(nc.sbuf_tensor("mpT_sb", [15, TOK], F32))
                wt_sb = _st2.enter_context(nc.sbuf_tensor("wt_sb", [8, TOK], F32))
                wb_sb = _st2.enter_context(nc.sbuf_tensor("wb_sb", [128, TOK], F32))
                ph_0 = _st2.enter_context(nc.psum_tensor("ph_0", [128, 512], F32))
                ph_1 = _st2.enter_context(nc.psum_tensor("ph_1", [128, 512], F32))
                plg_0 = _st2.enter_context(nc.psum_tensor("plg_0", [128, 512], F32))
                plg_1 = _st2.enter_context(nc.psum_tensor("plg_1", [128, 512], F32))
                ptrb_0 = _st2.enter_context(nc.psum_tensor("ptrb_0", [128, 128], BF16))
                ptrb_1 = _st2.enter_context(nc.psum_tensor("ptrb_1", [128, 128], BF16))
                ptrf = _st2.enter_context(nc.psum_tensor("ptrf", [128, 128], F32))
                pw = _st2.enter_context(nc.psum_tensor("pw", [128, 512], F32))
                ph = [ph_0, ph_1]
                plg = [plg_0, plg_1]
                ptrb = [ptrb_0, ptrb_1]

                @block.gpsimd
                def _(gpsimd):
                    gpsimd.wait_ge(sT, T_p0)
                    gpsimd.wait_ge(sV, V_p0)
                    gpsimd.wait_ge(sC, C_p0)
                    gpsimd.wait_ge(sSTG[0], 64)
                    gpsimd.wait_ge(sSTG[1], 64)
                    gpsimd.dma_start(out=em_i[:, :], in_=emap).then_inc(sG2, 16)
                    gpsimd.dma_start(out=a_raw[:, :], in_=A_in).then_inc(sG2, 16)
                    gpsimd.dma_start(out=rw_raw[:, :], in_=router_W).then_inc(sG2, 16)
                    gpsimd.dma_start(out=bp1[:, :], in_=b_proj).then_inc(sG2, 16)
                    gpsimd.dma_start(out=bb1[:, :], in_=base_b).then_inc(sG2, 16)
                    gpsimd.dma_start(out=rb_sb[:, :], in_=router_b).then_inc(sG2, 16)

                xo4 = xo_sb.rearrange("p f (j q) -> p f j q", j=NJ)

                @block.sync
                def _(sync):
                    sync.wait_ge(sT, T_p0)
                    sync.wait_ge(sV, V_p0)
                    sync.wait_ge(sC, C_p0)
                    sync.wait_ge(sSTG[0], 64)
                    sync.wait_ge(sSTG[1], 64)
                    for dc in range(CH):
                        if dc >= 4:
                            sync.wait_ge(sT, T_15dc[dc - 4])
                        sync.dma_start(
                            out=xo4[:, dc % 4, :, :],
                            in_=agi_x[:, dc, :, :],
                        ).then_inc(sXO[dc % 4], 16)

                @block.tensor
                def _(tensor):
                    tensor.wait_ge(sG2, G_p15)
                    tensor.wait_ge(sCON, CON0)
                    for dc in range(CH):
                        if dc >= 2:
                            tensor.wait_ge(sV, V_aTe[dc - 2])
                        tensor.transpose(
                            ptrb[dc % 2][:, :], a_raw[:, dc * 128:(dc + 1) * 128],
                            c_idb[:, :]).then_inc(sT, 1)
                    for dc in range(CH):
                        if dc >= 2:
                            tensor.wait_ge(sV, V_rwTe[dc - 2])
                        else:
                            tensor.wait_ge(sV, V_aTe[CH - 2 + dc])
                        tensor.transpose(
                            ptrb[dc % 2][:, 0:15], rw_raw[:, dc * 128:(dc + 1) * 128],
                            c_idb[0:15, 0:15]).then_inc(sT, 1)
                    for dc in range(CH):
                        tensor.wait_ge(sXO[dc % 4], 16 * (dc // 4 + 1))
                        tensor.wait_ge(sV, V_rwTe[dc])
                        for t2 in range(2):
                            tensor.matmul(
                                ph[t2][:, :], aT_sb[:, dc, :],
                                xo_sb[:, dc % 4, t2 * 512:(t2 + 1) * 512],
                                start=(dc == 0), stop=(dc == CH - 1))
                        for t2 in range(2):
                            mm = tensor.matmul(
                                plg[t2][0:15, :], rwT_sb[:, dc, :],
                                xo_sb[:, dc % 4, t2 * 512:(t2 + 1) * 512],
                                start=(dc == 0), stop=(dc == CH - 1))
                            if t2 == 1:
                                mm.then_inc(sT, 1)
                    for tt in range(8):
                        tensor.wait_ge(sV, V_lgev[1])
                        if tt >= 1:
                            tensor.wait_ge(sV, V_lgTe[tt - 1])
                        tensor.transpose(
                            ptrf[:, 0:15], lg_sb[:, tt * 128:(tt + 1) * 128],
                            c_idf[0:15, 0:15]).then_inc(sT, 1)
                    for tt in range(8):
                        tensor.wait_ge(sV, V_sm[tt])
                        if tt >= 1:
                            tensor.wait_ge(sV, V_mpTe[tt - 1])
                        else:
                            tensor.wait_ge(sV, V_lgTe[7])
                        tensor.transpose(
                            ptrf[0:15, 0:128], mp_all[:, tt, :], c_idf[:, :]
                        ).then_inc(sT, 1)
                    for t2 in range(2):
                        tensor.wait_ge(sV, V_mmap)
                        tensor.wait_ge(sV, V_mpTe[7])
                        if t2 == 1:
                            tensor.wait_ge(sV, V_wev[0])
                        tensor.matmul(
                            pw[0:8, :], mmap[:, :], mpT_sb[:, t2 * 512:(t2 + 1) * 512],
                            start=True, stop=True).then_inc(sT, 1)
                    for t2 in range(2):
                        tensor.wait_ge(sV, V_wev[1])
                        if t2 == 1:
                            tensor.wait_ge(sV, V_wbev[0])
                        tensor.matmul(
                            pw[:, :], c_expand[:, :], wt_sb[0:8, t2 * 512:(t2 + 1) * 512],
                            start=True, stop=True).then_inc(sT, 1)

                @block.vector
                def _(vector):
                    vector.wait_ge(sC, C_p0)
                    vector.wait_ge(sSTG[0], 64)
                    vector.wait_ge(sSTG[1], 64)
                    vector.wait_ge(sCON, CON0)
                    for dc in range(CH):
                        vector.wait_ge(sT, T_aT[dc])
                        vector.tensor_copy(aT_sb[:, dc, :], ptrb[dc % 2][:, :]).then_inc(sV, 1)
                    for dc in range(CH):
                        vector.wait_ge(sT, T_rwT[dc])
                        vector.tensor_copy(rwT_sb[:, dc, :], ptrb[dc % 2][:, 0:15]).then_inc(sV, 1)
                    vector.wait_ge(sG2, G_p15)
                    vector.tensor_copy(em_f[:, :], em_i[:, :])
                    vector.drain()
                    vector.tensor_scalar(mm_t[:, :], c_erange[:, :], em_f[:, :], None,
                                         op0=ALU.subtract).then_inc(sV, 1)
                    vector.wait_ge(sC, C_mmap)
                    vector.tensor_scalar(mm_sq2[:, :], mm_sq[:, :], -2.0, 2.0,
                                         op0=ALU.mult, op1=ALU.add)
                    vector.drain()
                    vector.tensor_scalar(mmap[:, :], mm_sq2[:, :], 0.0, None,
                                         op0=ALU.max).then_inc(sV, 1)
                    vector.wait_ge(sG2, G_p15)
                    vector.tensor_tensor(bias_bf[:, :], bp1[:, :], bb1[:, :], op=ALU.add).then_inc(sV, 1)
                    for t2 in range(2):
                        vector.wait_ge(sT, T_15dc[CH - 1])
                        vector.tensor_scalar(
                            lg_sb[:, t2 * 512:(t2 + 1) * 512], plg[t2][0:15, :],
                            rb_sb[:, :], None, op0=ALU.add).then_inc(sV, 1)
                    for tt in range(8):
                        vector.wait_ge(sT, T_lgT[tt])
                        vector.tensor_copy(lgT_sb[:, tt, :], ptrf[:, 0:15]).then_inc(sV, 1)
                    for tt in range(8):
                        vector.wait_ge(sV, V_lgTe[tt])
                        vector.reduce_max(sm_nm[:, :], lgT_sb[:, tt, :], axis=AX.X,
                                          negate=True).then_inc(sV, 1)
                        vector.wait_ge(sC, C_smexp[tt])
                        vector.reduce_sum(sm_s[:, :], sm_e[:, :], axis=AX.X)
                        vector.drain()
                        vector.reciprocal(sm_r[:, :], sm_s[:, :])
                        vector.drain()
                        vector.tensor_scalar(sm_p[:, :], sm_e[:, :], sm_r[:, :], None,
                                             op0=ALU.mult)
                        vector.drain()
                        vector.max(sm_m8[:, :], sm_p[:, :])
                        vector.drain()
                        vector.scalar_tensor_tensor(
                            mp_all[:, tt, :], sm_p[:, :], sm_m8[:, 7:8], sm_p[:, :],
                            op0=ALU.is_ge, op1=ALU.mult).then_inc(sV, 1)
                    for tt in range(8):
                        vector.wait_ge(sT, T_mpT[tt])
                        vector.tensor_copy(mpT_sb[:, tt * 128:(tt + 1) * 128],
                                           ptrf[0:15, 0:128]).then_inc(sV, 1)
                    for t2 in range(2):
                        vector.wait_ge(sT, T_wmm[t2])
                        vector.tensor_copy(wt_sb[0:8, t2 * 512:(t2 + 1) * 512],
                                           pw[0:8, :]).then_inc(sV, 1)
                    for t2 in range(2):
                        vector.wait_ge(sT, T_wbmm[t2])
                        vector.tensor_copy(wb_sb[:, t2 * 512:(t2 + 1) * 512],
                                           pw[:, :]).then_inc(sV, 1)
                    for t2 in range(2):
                        vector.wait_ge(sV, V_wbev[t2])
                        vector.tensor_tensor(
                            g_sb[:, t2 * 512:(t2 + 1) * 512], ph[t2][:, :],
                            wb_sb[:, t2 * 512:(t2 + 1) * 512], op=ALU.mult).then_inc(sV, 1)

                @block.scalar
                def _(scalar):
                    scalar.wait_ge(sV, V_mmt)
                    scalar.activation(mm_sq[:, :], mm_t[:, :], AF.Square).then_inc(sC, 1)
                    for tt in range(8):
                        scalar.wait_ge(sV, V_smn[tt])
                        scalar.activation(sm_e[:, :], lgT_sb[:, tt, :], AF.Exp,
                                          bias=sm_nm[:, :], scale=1.0).then_inc(sC, 1)

            with contextlib.ExitStack() as _st2:
                xt_sb = _st2.enter_context(nc.sbuf_tensor("xt_sb", [128, 2, CH, 512], BF16))
                ev_sb = _st2.enter_context(nc.sbuf_tensor("ev_sb", [128, 12, 512], BF16))
                p1_0 = _st2.enter_context(nc.psum_tensor("p1_0", [128, 512], F32))
                p1_1 = _st2.enter_context(nc.psum_tensor("p1_1", [128, 512], F32))
                p1_2 = _st2.enter_context(nc.psum_tensor("p1_2", [128, 512], F32))
                p1_3 = _st2.enter_context(nc.psum_tensor("p1_3", [128, 512], F32))
                p1_4 = _st2.enter_context(nc.psum_tensor("p1_4", [128, 512], F32))
                p1_5 = _st2.enter_context(nc.psum_tensor("p1_5", [128, 512], F32))
                p1_6 = _st2.enter_context(nc.psum_tensor("p1_6", [128, 512], F32))
                p1_7 = _st2.enter_context(nc.psum_tensor("p1_7", [128, 512], F32))
                banks = [p1_0, p1_1, p1_2, p1_3, p1_4, p1_5, p1_6, p1_7]

                @block.scalar
                def _(scalar):
                    scalar.wait_ge(sC, C_p15)
                    scalar.wait_ge(sV, V_p15)
                    scalar.wait_ge(sT, T_p15)
                    scalar.wait_ge(sSTG[0], 64)
                    scalar.wait_ge(sSTG[1], 64)
                    for i in range(4):
                        scalar.wait_ge(sXO[i], 16 * 8)
                    for tc in range(NTC):
                        h, s = tc // 8, tc % 8
                        p = tc % 2
                        scalar.wait_ge(sCC, h + 1)
                        if tc >= 2:
                            scalar.wait_ge(sT, T_p1[(tc - 2, 11)])
                        for c2 in range(2):
                            scalar.dma_start(
                                out=xt_sb[:, p, c2 * 16:(c2 + 1) * 16, :],
                                in_=ago_v[h, s, c2 * 16:(c2 + 1) * 16, :, :]
                                .rearrange("c p q -> p c q"),
                            ).then_inc(sX1[p], 16)

                @block.tensor
                def _(tensor):
                    tensor.wait_ge(sG, G_bw)
                    tensor.wait_ge(sONE, 1)
                    tensor.wait_ge(sV, V_p15)
                    for tc in range(NTC):
                        p = tc % 2
                        tensor.wait_ge(sX1[p], 32 * (tc // 2 + 1))
                        for jj in range(8):
                            if tc >= 1:
                                pk = 8 + jj if jj < 4 else jj
                                tensor.wait_ge(sV, V_p1[(tc - 1, pk)])
                            for dc in range(CH):
                                mm = tensor.matmul(
                                    banks[jj][:, :], wqk_sb[:, dc, jj * 128:(jj + 1) * 128],
                                    xt_sb[:, p, dc, :],
                                    start=(dc == 0), stop=(dc == CH - 1))
                                if dc == CH - 1:
                                    mm.then_inc(sT, 1)
                        for t4 in range(4):
                            tensor.wait_ge(sV, V_p1[(tc, t4)])
                            for dc in range(CH):
                                tensor.matmul(
                                    banks[t4][:, :], xt_sb[:, p, dc, t4 * 128:(t4 + 1) * 128],
                                    wv_sb[:, dc, :], start=(dc == 0), stop=False)
                            tensor.matmul(
                                banks[t4][:, :], ones_bf_r[:, :], bv_bf[:, :],
                                start=False, stop=True).then_inc(sT, 1)

                @block.vector
                def _(vector):
                    vector.wait_ge(sCON, CON0)
                    vector.wait_ge(sCONA, CONA)
                    for tc in range(NTC):
                        if tc >= 1:
                            vector.wait_ge(
                                sQK[(tc - 1) % 2], 192 * ((tc - 1) // 2 + 1))
                        for k in range(12):
                            vector.wait_ge(sT, T_p1[(tc, k)])
                            if k < 8:
                                vector.tensor_scalar(
                                    ev_sb[:, k, :], banks[k][:, :], bqk_sb[:, k:k + 1],
                                    None, op0=ALU.add).then_inc(sV, 1)
                            else:
                                vector.tensor_copy(
                                    ev_sb[:, k, :], banks[k - 8][:, :]).then_inc(sV, 1)

                @block.sync
                def _(sync):
                    for tc in range(NTC):
                        h, s = tc // 8, tc % 8
                        p = tc % 2
                        col = s * 1024 + h * 512
                        vrow = s * 8 + h * 4
                        for k in range(12):
                            sync.wait_ge(sV, V_p1[(tc, k)])
                            if k < 8:
                                dst = qT_loc if k < 4 else kT_loc
                                sync.dma_start(
                                    out=dst[k % 4, :, col:col + 512],
                                    in_=ev_sb[:, k, :],
                                ).then_inc(sQK[p], 16)
                            else:
                                sync.dma_start(
                                    out=v_loc[vrow + (k - 8), :, :],
                                    in_=ev_sb[:, k, :],
                                ).then_inc(sQK[p], 16)

        with contextlib.ExitStack() as _st2:
            c_et = _st2.enter_context(nc.sbuf_tensor("c_et", [128, 20, 512], BF16))
            q_sb = _st2.enter_context(nc.sbuf_tensor("q_sb", [128, 3, 4096], BF16))
            k_sb = _st2.enter_context(nc.sbuf_tensor("k_sb", [128, 3, 4096], BF16))
            v_sb = _st2.enter_context(nc.sbuf_tensor("v_sb", [128, 3, CH, 128], BF16))
            pe_sb = _st2.enter_context(nc.sbuf_tensor("pe_sb", [128, 4, 512], BF16))
            p_sb = _st2.enter_context(nc.sbuf_tensor("p_sb", [128, 4, 512], BF16))
            bc_sb = _st2.enter_context(nc.sbuf_tensor("bc_sb", [128, 2, 512], F32))
            onrm_sb = _st2.enter_context(nc.sbuf_tensor("onrm_sb", [128, 2, 512], BF16))
            wcv = _st2.enter_context(nc.sbuf_tensor("wcv", [128, 3, 2, D], BF16))
            ps_s0 = _st2.enter_context(nc.psum_tensor("ps_s0", [128, 512], F32))
            ps_s1 = _st2.enter_context(nc.psum_tensor("ps_s1", [128, 512], F32))
            ps_s2 = _st2.enter_context(nc.psum_tensor("ps_s2", [128, 512], F32))
            ps_s3 = _st2.enter_context(nc.psum_tensor("ps_s3", [128, 512], F32))
            ps_o0 = _st2.enter_context(nc.psum_tensor("ps_o0", [128, 512], F32))
            ps_o1 = _st2.enter_context(nc.psum_tensor("ps_o1", [128, 512], F32))
            ps_dn0 = _st2.enter_context(nc.psum_tensor("ps_dn0", [128, 512], F32))
            ps_dn1 = _st2.enter_context(nc.psum_tensor("ps_dn1", [128, 512], F32))
            ps_s = [ps_s0, ps_s1, ps_s2, ps_s3]
            ps_o = [ps_o0, ps_o1]
            ps_den = [ps_dn0, ps_dn1]

            def _p2_prefetch(sync, u):
                b, hh = u // 4, u % 4
                pu = u % 3
                sync.dma_start(
                    out=q_sb[:, pu, :], in_=qT_loc[hh, :, b * 4096:(b + 1) * 4096]
                ).then_inc(sU[pu], 16)
                sync.dma_start(
                    out=k_sb[:, pu, :], in_=kT_loc[hh, :, b * 4096:(b + 1) * 4096]
                ).then_inc(sU[pu], 16)
                for kq in range(4):
                    sync.dma_start(
                        out=v_sb[:, pu, kq * 8:(kq + 1) * 8, :],
                        in_=v_loc[b * 32 + kq * 8:b * 32 + (kq + 1) * 8, :,
                                  hh * 128:(hh + 1) * 128].rearrange("t p d -> p t d"),
                    ).then_inc(sU[pu], 16)

            @block.sync
            def _(sync):
                sync.wait_ge(sT, T_p1b)
                sync.wait_ge(sQK[0], QK_p1b)
                sync.wait_ge(sQK[1], QK_p1b)
                sync.wait_ge(sV, V_p1b)
                sync.wait_ge(sC, C_p15)
                sync.dma_start(out=c_et[:, :, :], in_=et_in).then_inc(sCON, 16)
                for u in range(3):
                    _p2_prefetch(sync, u)
                for u in range(NU):
                    b, hh = u // 4, u % 4
                    for qc in range(NQC):
                        g = u * NQC + qc
                        sync.wait_ge(sV, V_onrm[g])
                        shard = (b * 4096 + qc * 512) // 1024
                        sync.dma_start(
                            out=a2a_in_v[shard, hh * 128:(hh + 1) * 128,
                                         (qc % 2) * 512:(qc % 2) * 512 + 512],
                            in_=onrm_sb[:, g % 2, :],
                        ).then_inc(sA[g % 2], 16)
                    if u + 3 < NU:
                        sync.wait_ge(sT, T_pvf[(u) * NQC + NQC - 1])
                        _p2_prefetch(sync, u + 3)

            @block.tensor
            def _(tensor):
                for u in range(NU):
                    pu = u % 3
                    tensor.wait_ge(sU[pu], U_val[u])
                    for i in range(NPAIR + LAG):
                        if i < NPAIR:
                            gp = u * NPAIR + i
                            qc, kt = PAIRS[i][0], PAIRS[i][1]
                            if gp >= 4:
                                tensor.wait_ge(sC, C_exp[gp - 4])
                            tensor.matmul(
                                ps_s[gp % 4][:, :], k_sb[:, pu, kt * 128:(kt + 1) * 128],
                                q_sb[:, pu, qc * 512:(qc + 1) * 512],
                                start=True, stop=True).then_inc(sT, 1)
                        j = i - LAG
                        if j >= 0:
                            gj = u * NPAIR + j
                            qc, kt, nk, first, last = PAIRS[j]
                            g = u * NQC + qc
                            tensor.wait_ge(sV, V_stt[gj])
                            if first and g >= 2:
                                tensor.wait_ge(sV, V_rcp[g - 2])
                            mm = tensor.matmul(
                                ps_den[g % 2][:, :], ones_bf_sq[:, :], p_sb[:, gj % 4, :],
                                start=first, stop=last)
                            if last:
                                mm.then_inc(sT, 1)
                            if first and g >= 2:
                                tensor.wait_ge(sV, V_onrm[g - 2])
                            mm = tensor.matmul(
                                ps_o[g % 2][:, :], v_sb[:, pu, kt, :], p_sb[:, gj % 4, :],
                                start=first, stop=last)
                            if last:
                                mm.then_inc(sT, 1)

            @block.vector
            def _(vector):
                vector.wait_ge(sCON, CON1)
                for u in range(NU):
                    hh = u % 4
                    pend = []
                    for i in range(NPAIR + LAG + 2):
                        if i < NPAIR:
                            gp = u * NPAIR + i
                            qc, kt = PAIRS[i][0], PAIRS[i][1]
                            m = kt - 4 * qc
                            esel = hh * 5 + (0 if m < 0 else 1 + m)
                            vector.wait_ge(sC, C_exp[gp])
                            if gp >= 4:
                                vector.wait_ge(sT, T_S[gp])
                            vector.tensor_tensor(
                                p_sb[:, gp % 4, :], pe_sb[:, gp % 4, :],
                                c_et[:, esel, :], op=ALU.mult
                            ).then_inc(sV, 1)
                        j = i - LAG
                        if 0 <= j < NPAIR and PAIRS[j][4]:
                            pend.append((u * NQC + PAIRS[j][0], i + 2))
                        while pend and pend[0][1] <= i:
                            g = pend.pop(0)[0]
                            vector.wait_ge(sT, T_denf[g])
                            vector.reciprocal(
                                bc_sb[:, g % 2, :], ps_den[g % 2][:, :]).then_inc(sV, 1)
                            vector.drain()
                            vector.wait_ge(sT, T_pvf[g])
                            if g >= 2:
                                vector.wait_ge(sA[g % 2], 16 * (g // 2))
                            vector.tensor_tensor(
                                onrm_sb[:, g % 2, :], ps_o[g % 2][:, :], bc_sb[:, g % 2, :],
                                op=ALU.mult).then_inc(sV, 1)

            @block.scalar
            def _(scalar):
                for u in range(NU):
                    hh = u % 4
                    for i in range(NPAIR):
                        gp = u * NPAIR + i
                        qc, kt = PAIRS[i][0], PAIRS[i][1]
                        m = kt - 4 * qc
                        bidx = hh * 32 + (m + 28)
                        scalar.wait_ge(sT, T_S[gp])
                        if gp >= 4:
                            scalar.wait_ge(sV, V_stt[gp - 4])
                        scalar.activation(
                            pe_sb[:, gp % 4, :], ps_s[gp % 4][:, :], AF.Exp,
                            bias=c_bias[:, bidx:bidx + 1], scale=1.0 / 128.0
                        ).then_inc(sC, 1)

            @block.gpsimd
            def _(gpsimd):
                gpsimd.wait_ge(sT, T_p1b)
                gpsimd.wait_ge(sV, V_p1b)
                gpsimd.wait_ge(sC, C_p15)
                gpsimd.wait_ge(sQK[0], QK_p1b)
                gpsimd.wait_ge(sQK[1], QK_p1b)
                convs = []
                for (srcw, dstw) in ((base_WT, bwb), (w_proj, wpb)):
                    sv = srcw.rearrange("(c p) n -> p c n", p=128)
                    dv = dstw.rearrange("(c p) n -> p c n", p=128)
                    for ch in range(16):
                        convs.append((sv, dv, ch))
                for idx in range(len(convs) + 2):
                    if idx < len(convs):
                        sv, dv, ch = convs[idx]
                        sl = idx % 3
                        if idx >= 3:
                            gpsimd.wait_ge(sWCs[sl], 16 * ((idx - 3) // 3 + 1))
                        gpsimd.dma_start(
                            out=wcv[:, sl, :, :], in_=sv[:, 2 * ch:2 * ch + 2, :],
                        ).then_inc(sWCl[sl], 16)
                    jdx = idx - 2
                    if jdx >= 0:
                        sv, dv, ch = convs[jdx]
                        sl2 = jdx % 3
                        gpsimd.wait_ge(sWCl[sl2], 16 * (jdx // 3 + 1))
                        gpsimd.dma_start(
                            out=dv[:, 2 * ch:2 * ch + 2, :], in_=wcv[:, sl2, :, :],
                        ).then_inc(sWCs[sl2], 16)
                gpsimd.wait_ge(sA[0], 16 * 32)
                gpsimd.wait_ge(sA[1], 16 * 32)
                with allow_ici_all_to_all():
                    gpsimd.collective_compute(
                        "AllToAll", ALU.bypass, replica_groups=groups,
                        ins=[a2a_in[:, :]], outs=[a2a_out[:, :]],
                    ).then_inc(sCC, 1)

        with contextlib.ExitStack() as _st2:
            ao_sb = _st2.enter_context(nc.sbuf_tensor("ao_sb", [128, CH, 1024], BF16))
            wp_sb = _st2.enter_context(nc.sbuf_tensor("wp_sb", [128, 2, 16, 512], BF16))
            bwv_sb = _st2.enter_context(nc.sbuf_tensor("bwv_sb", [128, 2, 16, 512], BF16))
            x3_sb = _st2.enter_context(nc.sbuf_tensor("x3_sb", [128, 2, 4, 512], F32))
            xo3_sb = _st2.enter_context(nc.sbuf_tensor("xo3_sb", [128, 4, 1024], BF16))
            xo34 = xo3_sb.rearrange("p f (j q) -> p f j q", j=NJ)
            accs = _st2.enter_context(nc.sbuf_tensor("accs", [128, 2, 8, 512], BF16))
            out_sb = _st2.enter_context(nc.sbuf_tensor("out_sb", [128, 2, 512], F32))
            p3_0 = _st2.enter_context(nc.psum_tensor("p3_0", [128, 512], F32))
            p3_1 = _st2.enter_context(nc.psum_tensor("p3_1", [128, 512], F32))
            p3_2 = _st2.enter_context(nc.psum_tensor("p3_2", [128, 512], F32))
            p3_3 = _st2.enter_context(nc.psum_tensor("p3_3", [128, 512], F32))
            p3_4 = _st2.enter_context(nc.psum_tensor("p3_4", [128, 512], F32))
            p3_5 = _st2.enter_context(nc.psum_tensor("p3_5", [128, 512], F32))
            p3_6 = _st2.enter_context(nc.psum_tensor("p3_6", [128, 512], F32))
            p3_7 = _st2.enter_context(nc.psum_tensor("p3_7", [128, 512], F32))
            b3 = [p3_0, p3_1, p3_2, p3_3, p3_4, p3_5, p3_6, p3_7]

            @block.sync
            def _(sync):
                sync.wait_ge(sT, T_p2)
                sync.wait_ge(sV, V_p2)
                sync.wait_ge(sC, C_p2)
                sync.wait_ge(sWCs[0], 176)
                sync.wait_ge(sWCs[1], 176)
                sync.wait_ge(sWCs[2], 160)
                for oc in range(2):
                    sync.wait_ge(sV, V_a[(oc, 7)])
                    sync.dma_start(
                        out=acc_d[oc, :, :, :], in_=accs[:, oc % 2, :, :],
                    ).then_inc(sAC[oc % 2], 16)
                sync.wait_ge(sCC, NJ + 1)
                for cc in range(CH):
                    sync.dma_start(
                        out=ao_sb[:, cc, :],
                        in_=a2a_out_v[cc // 4, (cc % 4) * 128:(cc % 4 + 1) * 128, :],
                    ).then_inc(sAO, 16)
                for oc in range(2, 8):
                    sync.wait_ge(sV, V_a[(oc, 7)])
                    sync.dma_start(
                        out=acc_d[oc, :, :, :], in_=accs[:, oc % 2, :, :],
                    ).then_inc(sAC[oc % 2], 16)
                for oc in range(2):
                    sync.wait_ge(sAC[oc % 2], 64)
                    sync.dma_start(
                        out=accs[:, oc % 2, :, :], in_=acc_d[oc, :, :, :],
                    ).then_inc(sACL[oc % 2], 16)
                for oc in range(8):
                    for t8 in range(8):
                        idx = oc * 8 + t8
                        sync.wait_ge(sV, V_b[(oc, t8)])
                        sync.dma_start(
                            out=out_own[t8 * 128:(t8 + 1) * 128, oc * 512:(oc + 1) * 512],
                            in_=out_sb[:, t8 % 2, :],
                        ).then_inc(sOUT[idx % 2], 16)
                    if oc + 2 < 8:
                        sync.dma_start(
                            out=accs[:, oc % 2, :, :], in_=acc_d[oc + 2, :, :, :],
                        ).then_inc(sACL[oc % 2], 16)
                sync.wait_ge(sOUT[0], 16 * 32)
                sync.wait_ge(sOUT[1], 16 * 32)

            @block.gpsimd
            def _(gpsimd):
                gpsimd.wait_ge(sT, T_p2)
                gpsimd.wait_ge(sV, V_p2)
                gpsimd.wait_ge(sWCs[0], 176)
                gpsimd.wait_ge(sWCs[1], 176)
                gpsimd.wait_ge(sWCs[2], 160)
                for oc in range(8):
                    for half in range(2):
                        if oc >= 1:
                            gpsimd.wait_ge(sT, T_dc3[(oc - 1) * CH + half * 16 + 15])
                        gpsimd.dma_start(
                            out=bwv_sb[:, half, :, :],
                            in_=bwb.rearrange("(c p) o -> p c o", p=128)[
                                :, half * 16:(half + 1) * 16, oc * 512:(oc + 1) * 512],
                        ).then_inc(sW0, 16)
                for oc in range(8):
                    wdst = wp_sb if oc % 2 == 0 else bwv_sb
                    if oc == 1:
                        gpsimd.wait_ge(sT, T_a3[(7, 7)])
                    for half in range(2):
                        if oc >= 2:
                            gpsimd.wait_ge(sT, T_b3[(oc - 2, 7)])
                        gpsimd.dma_start(
                            out=wdst[:, half, :, :],
                            in_=wpb.rearrange("(c p) n -> p c n", p=128)[
                                :, half * 16:(half + 1) * 16, oc * 512:(oc + 1) * 512],
                        ).then_inc(sW1p[oc % 2], 16)

            @block.scalar
            def _(scalar):
                scalar.wait_ge(sT, T_p2)
                scalar.wait_ge(sV, V_p2)
                scalar.wait_ge(sWCs[0], 176)
                scalar.wait_ge(sWCs[1], 176)
                scalar.wait_ge(sWCs[2], 160)
                for oc in range(8):
                    for dc in range(CH):
                        gdc = oc * CH + dc
                        if gdc >= 4:
                            scalar.wait_ge(sT, T_dc3[gdc - 4])
                        scalar.dma_start(
                            out=xo34[:, dc % 4, :, :],
                            in_=agi_x[:, dc, :, :],
                        ).then_inc(sXO3[dc % 4], 16)
                for oc in range(8):
                    for half in range(2):
                        if oc >= 1:
                            scalar.wait_ge(sV, V_b[(oc - 1, half * 4 + 3)])
                        scalar.dma_start(
                            out=x3_sb[:, half, :, :],
                            in_=x_own.rearrange("(t p) n -> p t n", p=128)[
                                :, half * 4:(half + 1) * 4, oc * 512:(oc + 1) * 512],
                        ).then_inc(sX3[half], 16)

            @block.tensor
            def _(tensor):
                tensor.wait_ge(sV, V_p15)
                for oc in range(8):
                    tensor.wait_ge(sW0, 32 * (oc + 1))
                    for dc in range(CH):
                        gdc = oc * CH + dc
                        tensor.wait_ge(sXO3[dc % 4], 16 * (gdc // 4 + 1))
                        for t8 in range(8):
                            if dc == 0 and oc >= 1:
                                tensor.wait_ge(sV, V_a[(oc - 1, t8)])
                            mm = tensor.matmul(
                                b3[t8][:, :], xo3_sb[:, dc % 4, t8 * 128:(t8 + 1) * 128],
                                bwv_sb[:, dc // 16, dc % 16, :], start=(dc == 0), stop=False)
                            if t8 == 7:
                                mm.then_inc(sT, 1)
                    for t8 in range(8):
                        tensor.matmul(
                            b3[t8][:, :], g_sb[:, t8 * 128:(t8 + 1) * 128],
                            bp_sb[:, oc * 512:(oc + 1) * 512], start=False, stop=False)
                    for t8 in range(8):
                        tensor.matmul(
                            b3[t8][:, :], ones_bf_r[:, :],
                            bias_bf[:, oc * 512:(oc + 1) * 512], start=False,
                            stop=True).then_inc(sT, 1)
                tensor.wait_ge(sAO, 16 * 32)
                for oc in range(8):
                    wsrc = wp_sb if oc % 2 == 0 else bwv_sb
                    tensor.wait_ge(sW1p[oc % 2], 32 * (oc // 2 + 1))
                    for cc in range(CH):
                        for t8 in range(8):
                            if cc == 0:
                                tensor.wait_ge(
                                    sV, V_a[(7, t8)] if oc == 0 else V_b[(oc - 1, t8)])
                            mm = tensor.matmul(
                                b3[t8][:, :], ao_sb[:, cc, t8 * 128:(t8 + 1) * 128],
                                wsrc[:, cc // 16, cc % 16, :],
                                start=(cc == 0), stop=(cc == CH - 1))
                            if t8 == 7 and cc < CH - 1:
                                mm.then_inc(sT, 1)
                            if cc == CH - 1:
                                mm.then_inc(sT, 1)

            @block.vector
            def _(vector):
                for oc in range(8):
                    for t8 in range(8):
                        vector.wait_ge(sT, T_a3[(oc, t8)])
                        if oc >= 2 and t8 == 0:
                            vector.wait_ge(sAC[oc % 2], 16 * ((oc - 2) // 2 + 1))
                        vector.tensor_copy(
                            accs[:, oc % 2, t8, :], b3[t8][:, :]).then_inc(sV, 1)
                for oc in range(8):
                    for t8 in range(8):
                        idx = oc * 8 + t8
                        vector.wait_ge(sT, T_b3[(oc, t8)])
                        vector.wait_ge(sX3[t8 // 4], 16 * (oc + 1))
                        vector.wait_ge(sACL[oc % 2], 16 * (oc // 2 + 1))
                        if idx >= 2:
                            vector.wait_ge(sOUT[idx % 2], 16 * (idx // 2))
                        vector.tensor_tensor(
                            out_sb[:, t8 % 2, :], b3[t8][:, :], x3_sb[:, t8 // 4, t8 % 4, :],
                            op=ALU.add)
                        vector.drain()
                        vector.tensor_tensor(
                            out_sb[:, t8 % 2, :], out_sb[:, t8 % 2, :],
                            accs[:, oc % 2, t8, :], op=ALU.add).then_inc(sV, 1)

    return nc


_NC_CACHE = {}


def _get_program():
    if "nc" not in _NC_CACHE:
        _NC_CACHE["nc"] = build_program()
    return _NC_CACHE["nc"]


def _e_tiles(sl):
    ki = np.arange(128, dtype=np.float64)[:, None]
    qi = np.arange(512, dtype=np.float64)[None, :]
    out = np.zeros((128, 20, 512), dtype=np.float32)
    for hh in range(HPC):
        out[:, hh * 5, :] = np.exp(sl[hh] * (ki - qi - 128.0))
        for mm in range(4):
            out[:, hh * 5 + 1 + mm, :] = (
                np.exp(sl[hh] * (ki - qi + 128 * mm)) * ((ki + 128 * mm) <= qi))
    return out.astype(ml_dtypes.bfloat16)


def prep_in_maps(L0_attn_hidden_states, L0_attn_w_attn, L0_attn_b_attn,
                 L0_attn_w_proj, L0_attn_b_proj, L0_ffn_base_W, L0_ffn_base_b,
                 L0_ffn_router_W, L0_ffn_router_b, L0_ffn_A, L0_ffn_B,
                 L0_ffn_expert_map):
    x = np.ascontiguousarray(L0_attn_hidden_states.reshape(NT, D))
    w_attn = np.asarray(L0_attn_w_attn)
    b_attn = np.asarray(L0_attn_b_attn)
    slopes = _alibi_slopes(32)
    idb = np.eye(128, dtype=ml_dtypes.bfloat16)
    idf = np.eye(128, dtype=np.float32)
    er = np.broadcast_to(np.arange(8, dtype=np.float32)[None, :], (15, 8)).copy()
    ex = np.zeros((8, 128), dtype=np.float32)
    for e in range(8):
        ex[e, e * 16:(e + 1) * 16] = 1.0

    base_WT = np.ascontiguousarray(np.asarray(L0_ffn_base_W).T)
    Bt = np.ascontiguousarray(
        np.asarray(L0_ffn_B).transpose(0, 2, 1).reshape(128, D))
    in_maps = []
    for c in range(NCORE):
        qcols = slice(4 * c * 128, 4 * c * 128 + 512)
        kcols = slice(4096 + 4 * c * 128, 4096 + 4 * c * 128 + 512)
        vcols = slice(8192 + 4 * c * 128, 8192 + 4 * c * 128 + 512)
        wqk_c = np.ascontiguousarray(
            np.concatenate([w_attn[:, qcols], w_attn[:, kcols]], axis=1))
        wv_c = np.ascontiguousarray(w_attn[:, vcols])
        bqk_c = np.ascontiguousarray(
            np.concatenate([b_attn[qcols], b_attn[kcols]]).reshape(8, 128).T)
        bv_c = np.ascontiguousarray(b_attn[vcols].reshape(1, 512))
        sl = slopes[4 * c:4 * c + 4]
        slope128 = np.broadcast_to(128.0 * sl[None, :], (128, HPC)).copy()
        btab = np.zeros((128, HPC * 32), dtype=np.float32)
        for hh in range(HPC):
            for mi in range(28):
                btab[:, hh * 32 + mi] = sl[hh] * 128.0 * (mi - 27)
        in_maps.append({
            "x_own": np.ascontiguousarray(x[c * TOK:(c + 1) * TOK]),
            "wqk": wqk_c, "wv_in": wv_c, "bqk_in": bqk_c, "bv_in": bv_c,
            "w_proj": np.asarray(L0_attn_w_proj),
            "b_proj": np.asarray(L0_attn_b_proj).reshape(1, D),
            "base_WT": base_WT,
            "base_b": np.asarray(L0_ffn_base_b).reshape(1, D),
            "router_W": np.asarray(L0_ffn_router_W),
            "router_b": np.asarray(L0_ffn_router_b).reshape(15, 1),
            "A_in": np.ascontiguousarray(np.asarray(L0_ffn_A).reshape(128, D)),
            "B_in": Bt,
            "emap": np.asarray(L0_ffn_expert_map).reshape(15, 1).astype(np.int32),
            "et_in": _e_tiles(sl), "slope_in": slope128, "btab_in": btab,
            "idb_in": idb, "idf_in": idf, "er_in": er, "ex_in": ex,
        })
    return in_maps


def kernel(**inputs):
    in_maps = prep_in_maps(**inputs)
    nc = _get_program()
    res = run_bass_kernel_spmd(nc, in_maps, core_ids=list(range(NCORE)))
    out = np.empty((NT, D), dtype=np.float32)
    for c in range(NCORE):
        out[c * TOK:(c + 1) * TOK] = res.results[c]["out_own"]
    return out.reshape(2, 4096, 4096)


# version 13
# speedup vs baseline: 1.0365x; 1.0365x over previous
import math
import contextlib
import numpy as np
import ml_dtypes
import concourse.bass as bass
from concourse.aot_env import allow_ici_all_to_all
import concourse.mybir as mybir
from concourse.bass_utils import run_bass_kernel_spmd

F32 = mybir.dt.float32
BF16 = mybir.dt.bfloat16
I32 = mybir.dt.int32
AF = mybir.ActivationFunctionType
ALU = mybir.AluOpType
AX = mybir.AxisListType

D = 4096
NCORE, HPC = 8, 4
TOK, NT = 1024, 8192
CH = 32
NTC = 16
NU, NQC = 8, 8
EPS = 1e-5
NEGBIG = -1.0e13
LAG = 3

PAIRS = []
for _qc in range(NQC):
    _nk = 4 * (_qc + 1)
    for _kt in range(_nk):
        PAIRS.append((_qc, _kt, _nk, _kt == 0, _kt == _nk - 1))
NPAIR = len(PAIRS)


def _alibi_slopes(n_heads):
    cp2 = 2 ** math.floor(math.log2(n_heads))
    base = 2 ** (-(2 ** (-(math.log2(cp2) - 3))))
    slopes = base ** np.arange(1, cp2 + 1, dtype=np.float32)
    if cp2 != n_heads:
        extra_base = 2 ** (-(2 ** (-(math.log2(2 * cp2) - 3))))
        n_rem = min(cp2, n_heads - cp2)
        extra = extra_base ** np.arange(1, 1 + 2 * n_rem, 2, dtype=np.float32)
        slopes = np.concatenate([slopes, extra])
    return slopes.astype(np.float32)


def build_program():
    nc = bass.Bass("TRN2", target_bir_lowering=False)

    x_own = nc.dram_tensor("x_own", [TOK, D], F32, kind="ExternalInput").ap()
    wqk = nc.dram_tensor("wqk", [D, 1024], F32, kind="ExternalInput").ap()
    wv_in = nc.dram_tensor("wv_in", [D, 512], F32, kind="ExternalInput").ap()
    bqk_in = nc.dram_tensor("bqk_in", [128, 8], F32, kind="ExternalInput").ap()
    bv_in = nc.dram_tensor("bv_in", [1, 512], F32, kind="ExternalInput").ap()
    w_proj = nc.dram_tensor("w_proj", [D, D], F32, kind="ExternalInput").ap()
    b_proj = nc.dram_tensor("b_proj", [1, D], F32, kind="ExternalInput").ap()
    base_WT = nc.dram_tensor("base_WT", [D, D], F32, kind="ExternalInput").ap()
    base_b = nc.dram_tensor("base_b", [1, D], F32, kind="ExternalInput").ap()
    router_W = nc.dram_tensor("router_W", [15, D], F32, kind="ExternalInput").ap()
    router_b = nc.dram_tensor("router_b", [15, 1], F32, kind="ExternalInput").ap()
    A_in = nc.dram_tensor("A_in", [128, D], F32, kind="ExternalInput").ap()
    B_in = nc.dram_tensor("B_in", [128, D], F32, kind="ExternalInput").ap()
    emap = nc.dram_tensor("emap", [15, 1], I32, kind="ExternalInput").ap()
    et_in = nc.dram_tensor("et_in", [128, 20, 512], BF16, kind="ExternalInput").ap()
    slope_in = nc.dram_tensor("slope_in", [128, HPC], F32, kind="ExternalInput").ap()
    btab_in = nc.dram_tensor("btab_in", [128, HPC * 32], F32, kind="ExternalInput").ap()
    idb_in = nc.dram_tensor("idb_in", [128, 128], BF16, kind="ExternalInput").ap()
    idf_in = nc.dram_tensor("idf_in", [128, 128], F32, kind="ExternalInput").ap()
    er_in = nc.dram_tensor("er_in", [15, 8], F32, kind="ExternalInput").ap()
    ex_in = nc.dram_tensor("ex_in", [8, 128], F32, kind="ExternalInput").ap()
    out_own = nc.dram_tensor("out_own", [TOK, D], F32, kind="ExternalOutput").ap()

    NJ = 4
    agi = nc.dram_tensor("agi", [NJ * D, 256], BF16).ap()
    ago = nc.dram_tensor("ago", [NJ * NCORE * D, 256], BF16).ap()
    qT_loc = nc.dram_tensor("qT_loc", [HPC, 128, NT], BF16).ap()
    kT_loc = nc.dram_tensor("kT_loc", [HPC, 128, NT], BF16).ap()
    v_loc = nc.dram_tensor("v_loc", [NT // 128, 128, 512], BF16).ap()
    a2a_in = nc.dram_tensor("a2a_in", [NCORE * 512, TOK], BF16).ap()
    a2a_out = nc.dram_tensor("a2a_out", [NCORE * 512, TOK], BF16).ap()
    acc_d = nc.dram_tensor("acc_d", [8, 128, 8, 512], BF16).ap()
    wpb = nc.dram_tensor("wpb", [D, D], BF16).ap()
    bwb = nc.dram_tensor("bwb", [D, D], BF16).ap()

    agi_v = agi.rearrange("(j c p) q -> j c p q", j=NJ, p=128)
    agi_x = agi.rearrange("(j c p) q -> p c j q", j=NJ, p=128)
    ago_v = ago.rearrange("(j s c p) q -> j s c p q", j=NJ, s=NCORE, p=128)
    a2a_in_v = a2a_in.rearrange("(s c) q -> s c q", s=NCORE)
    a2a_out_v = a2a_out.rearrange("(s c) q -> s c q", s=NCORE)
    groups = [list(range(NCORE))]

    with contextlib.ExitStack() as _st:
        ec = _st.enter_context
        c_slope = ec(nc.sbuf_tensor("c_slope", [128, HPC], F32))
        c_bias = ec(nc.sbuf_tensor("c_bias", [128, HPC * 32], F32))
        c_idb = ec(nc.sbuf_tensor("c_idb", [128, 128], BF16))
        c_idf = ec(nc.sbuf_tensor("c_idf", [128, 128], F32))
        c_erange = ec(nc.sbuf_tensor("c_erange", [15, 8], F32))
        c_expand = ec(nc.sbuf_tensor("c_expand", [8, 128], F32))
        ones_bf_r = ec(nc.sbuf_tensor("ones_bf_r", [1, 128], BF16))
        ones_bf_sq = ec(nc.sbuf_tensor("ones_bf_sq", [128, 128], BF16))
        eps_sb = ec(nc.sbuf_tensor("eps_sb", [128, 1], F32))
        g_sb = ec(nc.sbuf_tensor("g_sb", [128, TOK], BF16))
        bp_sb = ec(nc.sbuf_tensor("bp_sb", [128, D], BF16))
        bias_bf = ec(nc.sbuf_tensor("bias_bf", [1, D], BF16))
        bv_bf = ec(nc.sbuf_tensor("bv_bf", [1, 512], BF16))
        sG = ec(nc.semaphore("sG"))
        sG2 = ec(nc.semaphore("sG2"))
        sT = ec(nc.semaphore("sT"))
        sV = ec(nc.semaphore("sV"))
        sC = ec(nc.semaphore("sC"))
        sCC = ec(nc.semaphore("sCC"))
        sCON = ec(nc.semaphore("sCON"))
        sCONA = ec(nc.semaphore("sCONA"))
        sSTG = [ec(nc.semaphore(f"sSTG{i}")) for i in range(2)]
        sAO = ec(nc.semaphore("sAO"))
        sX = [ec(nc.semaphore(f"sX{i}")) for i in range(2)]
        sXL = [ec(nc.semaphore(f"sXL{i}")) for i in range(3)]
        sX1 = [ec(nc.semaphore(f"sX1{i}")) for i in range(2)]
        sQK = [ec(nc.semaphore(f"sQK{i}")) for i in range(2)]
        sXO = [ec(nc.semaphore(f"sXO{i}")) for i in range(4)]
        sU = [ec(nc.semaphore(f"sU{i}")) for i in range(3)]
        sA = [ec(nc.semaphore(f"sA{i}")) for i in range(2)]
        sW0 = ec(nc.semaphore("sW0"))
        sW1 = ec(nc.semaphore("sW1"))
        sXO3 = [ec(nc.semaphore(f"sXO3{i}")) for i in range(4)]
        sAC = [ec(nc.semaphore(f"sAC{i}")) for i in range(2)]
        sWCl = [ec(nc.semaphore(f"sWCl{i}")) for i in range(3)]
        sWCs = [ec(nc.semaphore(f"sWCs{i}")) for i in range(3)]
        sACL = [ec(nc.semaphore(f"sACL{i}")) for i in range(2)]
        sX3 = [ec(nc.semaphore(f"sX3{i}")) for i in range(2)]
        sOUT = [ec(nc.semaphore(f"sOUT{i}")) for i in range(2)]
        sONE = ec(nc.semaphore("sONE"))
        block = ec(nc.Block())

        G_bw = 16 * 8
        G_p15 = 96
        CONA = 16 * 4
        CON0 = 16 * 3
        CON1 = CON0 + 16

        V_bn, V_ag, V_nmr, V_cp = {}, {}, {}, {}
        _vr = 0
        for tt in range(8):
            _vr += 1; V_bn[tt] = _vr
            _vr += 1; V_ag[tt] = _vr
            _vr += 1; V_nmr[tt] = _vr
            if tt >= 1:
                for g in range(4):
                    _vr += 1; V_cp[(tt - 1, g)] = _vr
        for g in range(4):
            _vr += 1; V_cp[(7, g)] = _vr
        V_p0 = _vr
        C_rstd = {tt: 2 * tt + 1 for tt in range(8)}
        C_norm = {tt: 2 * tt + 2 for tt in range(8)}
        C_p0 = 16
        T_tr = {(tt, g): 4 * tt + g + 1 for tt in range(8) for g in range(4)}
        T_p0 = 32

        T_aT = {dc: T_p0 + dc + 1 for dc in range(CH)}
        T_rwT = {dc: T_p0 + 32 + dc + 1 for dc in range(CH)}
        T_15dc = {dc: T_p0 + 64 + dc + 1 for dc in range(CH)}
        T_lgT = {tt: T_p0 + 96 + tt + 1 for tt in range(8)}
        T_mpT = {tt: T_p0 + 104 + tt + 1 for tt in range(8)}
        T_wmm = {t2: T_p0 + 112 + t2 + 1 for t2 in range(2)}
        T_wbmm = {t2: T_p0 + 114 + t2 + 1 for t2 in range(2)}
        T_p15 = T_p0 + 116
        V_aTe = {dc: V_p0 + dc + 1 for dc in range(CH)}
        V_rwTe = {dc: V_p0 + 32 + dc + 1 for dc in range(CH)}
        V_mmt = V_p0 + 65
        V_mmap = V_p0 + 66
        V_bsum = V_p0 + 67
        V_lgev = {t2: V_p0 + 67 + t2 + 1 for t2 in range(2)}
        V_lgTe = {tt: V_p0 + 69 + tt + 1 for tt in range(8)}
        V_smn = {tt: V_p0 + 77 + 2 * tt + 1 for tt in range(8)}
        V_sm = {tt: V_p0 + 77 + 2 * tt + 2 for tt in range(8)}
        V_mpTe = {tt: V_p0 + 93 + tt + 1 for tt in range(8)}
        V_wev = {t2: V_p0 + 101 + t2 + 1 for t2 in range(2)}
        V_wbev = {t2: V_p0 + 103 + t2 + 1 for t2 in range(2)}
        V_g = {t2: V_p0 + 105 + t2 + 1 for t2 in range(2)}
        V_p15 = V_p0 + 107
        C_mmap = C_p0 + 1
        C_smexp = {tt: C_p0 + 1 + tt + 1 for tt in range(8)}
        C_p15 = C_p0 + 9

        T_p1 = {(tc, k): T_p15 + 12 * tc + k + 1 for tc in range(NTC) for k in range(12)}
        V_p1 = {(tc, k): V_p15 + 12 * tc + k + 1 for tc in range(NTC) for k in range(12)}
        T_p1b = T_p15 + 12 * NTC
        V_p1b = V_p15 + 12 * NTC
        QK_p1b = 16 * 96

        T_S, T_denf, T_pvf = {}, {}, {}
        t_run = T_p1b
        for u in range(NU):
            for i in range(NPAIR + LAG):
                if i < NPAIR:
                    t_run += 1
                    T_S[u * NPAIR + i] = t_run
                j = i - LAG
                if j >= 0 and PAIRS[j][4]:
                    g = u * NQC + PAIRS[j][0]
                    t_run += 1; T_denf[g] = t_run
                    t_run += 1; T_pvf[g] = t_run
        T_p2 = t_run
        V_stt, V_rcp, V_onrm = {}, {}, {}
        v_run = V_p1b
        for u in range(NU):
            _pend = []
            for i in range(NPAIR + LAG + 2):
                if i < NPAIR:
                    v_run += 1
                    V_stt[u * NPAIR + i] = v_run
                j = i - LAG
                if 0 <= j < NPAIR and PAIRS[j][4]:
                    _pend.append((u * NQC + PAIRS[j][0], i + 2))
                while _pend and _pend[0][1] <= i:
                    g = _pend.pop(0)[0]
                    v_run += 1; V_rcp[g] = v_run
                    v_run += 1; V_onrm[g] = v_run
        V_p2 = v_run
        C_exp = {u * NPAIR + i: C_p15 + u * NPAIR + i + 1
                 for u in range(NU) for i in range(NPAIR)}
        C_p2 = C_p15 + NU * NPAIR
        U_val = {u: 96 * (u // 3 + 1) for u in range(NU)}

        T_dc3, T_a3, T_cc3, T_b3 = {}, {}, {}, {}
        t_run = T_p2
        for oc in range(8):
            for dc in range(CH):
                t_run += 1
                T_dc3[oc * CH + dc] = t_run
            for t8 in range(8):
                t_run += 1
                T_a3[(oc, t8)] = t_run
        for oc in range(8):
            for cc in range(CH - 1):
                t_run += 1
                T_cc3[oc * CH + cc] = t_run
            for t8 in range(8):
                t_run += 1
                T_b3[(oc, t8)] = t_run
        T_p3e = t_run
        V_a = {(oc, t8): V_p2 + 8 * oc + t8 + 1 for oc in range(8) for t8 in range(8)}
        V_ae = V_p2 + 64
        V_b = {(oc, t8): V_ae + 8 * oc + t8 + 1 for oc in range(8) for t8 in range(8)}
        V_p3e = V_ae + 64

        with contextlib.ExitStack() as _st2:
            wqk_sb = _st2.enter_context(nc.sbuf_tensor("wqk_sb", [128, CH, 1024], BF16))
            wv_sb = _st2.enter_context(nc.sbuf_tensor("wv_sb", [128, CH, 512], BF16))
            bqk_sb = _st2.enter_context(nc.sbuf_tensor("bqk_sb", [128, 8], F32))
            @block.gpsimd
            def _(gpsimd):
                gpsimd.dma_start(out=bp_sb[:, :], in_=B_in).then_inc(sG, 16)
                gpsimd.dma_start(out=bv_bf[:, :], in_=bv_in).then_inc(sG, 16)
                for i in range(4):
                    gpsimd.dma_start(
                        out=wqk_sb[:, i * 8:(i + 1) * 8, :],
                        in_=wqk.rearrange("(c p) n -> p c n", p=128)[:, i * 8:(i + 1) * 8, :],
                    ).then_inc(sG, 16)
                for i in range(2):
                    gpsimd.dma_start(
                        out=wv_sb[:, i * 16:(i + 1) * 16, :],
                        in_=wv_in.rearrange("(c p) n -> p c n", p=128)[:, i * 16:(i + 1) * 16, :],
                    ).then_inc(sG, 16)
                gpsimd.memset(ones_bf_r[:, :], 1.0)
                gpsimd.memset(ones_bf_sq[:, :], 1.0)
                gpsimd.memset(eps_sb[:, :], EPS).then_inc(sONE, 1)

            @block.sync
            def _(sync):
                for ap_d, ap_s in [
                    (c_slope[:, :], slope_in), (c_bias[:, :], btab_in),
                    (c_idb[:, :], idb_in), (c_idf[:, :], idf_in),
                ]:
                    sync.dma_start(out=ap_d, in_=ap_s).then_inc(sCONA, 16)
                for ap_d, ap_s in [
                    (c_erange[:, :], er_in), (c_expand[:, :], ex_in),
                    (bqk_sb[:, :], bqk_in),
                ]:
                    sync.dma_start(out=ap_d, in_=ap_s).then_inc(sCON, 16)

            with contextlib.ExitStack() as _st2:
                x_t = _st2.enter_context(nc.sbuf_tensor("x_t", [128, 3, D], F32))
                xn_t = _st2.enter_context(nc.sbuf_tensor("xn_t", [128, 2, D], BF16))
                stg = _st2.enter_context(nc.sbuf_tensor("stg", [128, 2, CH, 128], BF16))
                st6 = _st2.enter_context(nc.sbuf_tensor("st6", [128, 2, 48], F32))
                mv_sb = _st2.enter_context(nc.sbuf_tensor("mv_sb", [128, 2, 2], F32))
                r_sb = _st2.enter_context(nc.sbuf_tensor("r_sb", [128, 2], F32))
                sq_sb = _st2.enter_context(nc.sbuf_tensor("sq_sb", [128, 2], F32))
                nmr_sb = _st2.enter_context(nc.sbuf_tensor("nmr_sb", [128, 2], F32))
                p0_trA = _st2.enter_context(nc.psum_tensor("p0_trA", [128, 8, 128], BF16))
                p0_trB = _st2.enter_context(nc.psum_tensor("p0_trB", [128, 8, 128], BF16))
                p0_tr = [p0_trA, p0_trB]

                @block.sync
                def _(sync):
                    for tt in range(8):
                        px = tt % 3
                        if tt >= 3:
                            sync.wait_ge(sV, V_bn[tt - 3])
                            sync.wait_ge(sC, C_norm[tt - 3])
                        sync.dma_start(
                            out=x_t[:, px, :], in_=x_own[tt * 128:(tt + 1) * 128, :]
                        ).then_inc(sXL[px], 16)

                @block.gpsimd
                def _(gpsimd):
                    for j in range(NJ):
                        for tt in range(2 * j, 2 * j + 2):
                            gpsimd.wait_ge(sV, V_cp[(tt, 3)])
                            gpsimd.dma_start(
                                out=agi_v[tt // 2, :, :, (tt % 2) * 128:(tt % 2) * 128 + 128]
                                .rearrange("c p q -> p c q"),
                                in_=stg[:, tt % 2, :, :],
                            ).then_inc(sSTG[tt % 2], 16)
                        gpsimd.wait_ge(sSTG[0], 16 * (j + 1))
                        gpsimd.wait_ge(sSTG[1], 16 * (j + 1))
                        gpsimd.collective_compute(
                            "AllGather", ALU.bypass, replica_groups=groups,
                            ins=[agi[j * D:(j + 1) * D, :]],
                            outs=[ago[j * NCORE * D:(j + 1) * NCORE * D, :]],
                        ).then_inc(sCC, 1)

                def _p0_copies(vector, ct):
                    cp = ct % 2
                    for g in range(4):
                        vector.wait_ge(sT, T_tr[(ct, g)])
                        if ct >= 2 and g == 0:
                            vector.wait_ge(sSTG[cp], 16 * (ct // 2))
                        vector.tensor_copy(
                            stg[:, cp, g * 8:(g + 1) * 8, :],
                            p0_tr[(4 * ct + g) % 2][:, :, :]
                        ).then_inc(sV, 1)

                @block.vector
                def _(vector):
                    for tt in range(8):
                        p = tt % 2
                        px = tt % 3
                        vector.wait_ge(sXL[px], 16 * (tt // 3 + 1))
                        for sl in range(8):
                            bs = vector.bn_stats(
                                st6[:, p, 6 * sl:6 * sl + 6],
                                x_t[:, px, 512 * sl:512 * (sl + 1)])
                            if sl == 7:
                                bs.then_inc(sV, 1)
                        vector.drain()
                        if tt >= 2:
                            vector.wait_ge(sC, C_rstd[tt - 2])
                        vector.bn_aggr(mv_sb[:, p, :], st6[:, p, :]).then_inc(sV, 1)
                        vector.wait_ge(sC, C_rstd[tt])
                        if tt >= 2:
                            vector.wait_ge(sC, C_norm[tt - 2])
                        vector.reciprocal(r_sb[:, p:p + 1], sq_sb[:, p:p + 1])
                        vector.drain()
                        vector.scalar_tensor_tensor(
                            nmr_sb[:, p:p + 1], mv_sb[:, p, 0:1], -1.0, r_sb[:, p:p + 1],
                            op0=ALU.mult, op1=ALU.mult).then_inc(sV, 1)
                        if tt >= 1:
                            _p0_copies(vector, tt - 1)
                    _p0_copies(vector, 7)

                @block.scalar
                def _(scalar):
                    scalar.wait_ge(sONE, 1)
                    scalar.wait_ge(sCONA, CONA)
                    for tt in range(8):
                        p = tt % 2
                        scalar.wait_ge(sV, V_ag[tt])
                        scalar.activation(
                            sq_sb[:, p:p + 1], mv_sb[:, p, 1:2], AF.Sqrt,
                            bias=eps_sb[:, :], scale=1.0).then_inc(sC, 1)
                        scalar.wait_ge(sV, V_nmr[tt])
                        if tt >= 2:
                            scalar.wait_ge(sT, T_tr[(tt - 2, 3)])
                        scalar.activation(
                            xn_t[:, p, :], x_t[:, tt % 3, :], AF.Identity,
                            bias=nmr_sb[:, p:p + 1], scale=r_sb[:, p:p + 1]
                        ).then_inc(sC, 1)

                @block.tensor
                def _(tensor):
                    tensor.wait_ge(sCONA, CONA)
                    for tt in range(8):
                        p = tt % 2
                        tensor.wait_ge(sC, C_norm[tt])
                        for g in range(4):
                            gg = 4 * tt + g
                            if gg >= 2:
                                ptt, pg = divmod(gg - 2, 4)
                                tensor.wait_ge(sV, V_cp[(ptt, pg)])
                            for k in range(8):
                                dc = g * 8 + k
                                mm = tensor.transpose(
                                    p0_tr[gg % 2][:, k, :],
                                    xn_t[:, p, dc * 128:(dc + 1) * 128],
                                    c_idb[:, :])
                                if k == 7:
                                    mm.then_inc(sT, 1)

            with contextlib.ExitStack() as _st2:
                a_raw = _st2.enter_context(nc.sbuf_tensor("a_raw", [128, D], BF16))
                rw_raw = _st2.enter_context(nc.sbuf_tensor("rw_raw", [15, D], BF16))
                aT_sb = _st2.enter_context(nc.sbuf_tensor("aT_sb", [128, CH, 128], BF16))
                rwT_sb = _st2.enter_context(nc.sbuf_tensor("rwT_sb", [128, CH, 15], BF16))
                xo_sb = _st2.enter_context(nc.sbuf_tensor("xo_sb", [128, 4, 1024], BF16))
                em_i = _st2.enter_context(nc.sbuf_tensor("em_i", [15, 1], I32))
                em_f = _st2.enter_context(nc.sbuf_tensor("em_f", [15, 1], F32))
                mm_t = _st2.enter_context(nc.sbuf_tensor("mm_t", [15, 8], F32))
                mm_sq = _st2.enter_context(nc.sbuf_tensor("mm_sq", [15, 8], F32))
                mm_sq2 = _st2.enter_context(nc.sbuf_tensor("mm_sq2", [15, 8], F32))
                mmap = _st2.enter_context(nc.sbuf_tensor("mmap", [15, 8], F32))
                bp1 = _st2.enter_context(nc.sbuf_tensor("bp1", [1, D], BF16))
                bb1 = _st2.enter_context(nc.sbuf_tensor("bb1", [1, D], BF16))
                rb_sb = _st2.enter_context(nc.sbuf_tensor("rb_sb", [15, 1], F32))
                lg_sb = _st2.enter_context(nc.sbuf_tensor("lg_sb", [15, TOK], F32))
                lgT_sb = _st2.enter_context(nc.sbuf_tensor("lgT_sb", [128, 8, 15], F32))
                sm_nm = _st2.enter_context(nc.sbuf_tensor("sm_nm", [128, 1], F32))
                sm_e = _st2.enter_context(nc.sbuf_tensor("sm_e", [128, 15], F32))
                sm_s = _st2.enter_context(nc.sbuf_tensor("sm_s", [128, 1], F32))
                sm_r = _st2.enter_context(nc.sbuf_tensor("sm_r", [128, 1], F32))
                sm_p = _st2.enter_context(nc.sbuf_tensor("sm_p", [128, 15], F32))
                sm_m8 = _st2.enter_context(nc.sbuf_tensor("sm_m8", [128, 8], F32))
                mp_all = _st2.enter_context(nc.sbuf_tensor("mp_all", [128, 8, 15], F32))
                mpT_sb = _st2.enter_context(nc.sbuf_tensor("mpT_sb", [15, TOK], F32))
                wt_sb = _st2.enter_context(nc.sbuf_tensor("wt_sb", [8, TOK], F32))
                wb_sb = _st2.enter_context(nc.sbuf_tensor("wb_sb", [128, TOK], F32))
                ph_0 = _st2.enter_context(nc.psum_tensor("ph_0", [128, 512], F32))
                ph_1 = _st2.enter_context(nc.psum_tensor("ph_1", [128, 512], F32))
                plg_0 = _st2.enter_context(nc.psum_tensor("plg_0", [128, 512], F32))
                plg_1 = _st2.enter_context(nc.psum_tensor("plg_1", [128, 512], F32))
                ptrb_0 = _st2.enter_context(nc.psum_tensor("ptrb_0", [128, 128], BF16))
                ptrb_1 = _st2.enter_context(nc.psum_tensor("ptrb_1", [128, 128], BF16))
                ptrf = _st2.enter_context(nc.psum_tensor("ptrf", [128, 128], F32))
                pw = _st2.enter_context(nc.psum_tensor("pw", [128, 512], F32))
                ph = [ph_0, ph_1]
                plg = [plg_0, plg_1]
                ptrb = [ptrb_0, ptrb_1]

                @block.gpsimd
                def _(gpsimd):
                    gpsimd.wait_ge(sT, T_p0)
                    gpsimd.wait_ge(sV, V_p0)
                    gpsimd.wait_ge(sC, C_p0)
                    gpsimd.wait_ge(sSTG[0], 64)
                    gpsimd.wait_ge(sSTG[1], 64)
                    gpsimd.dma_start(out=em_i[:, :], in_=emap).then_inc(sG2, 16)
                    gpsimd.dma_start(out=a_raw[:, :], in_=A_in).then_inc(sG2, 16)
                    gpsimd.dma_start(out=rw_raw[:, :], in_=router_W).then_inc(sG2, 16)
                    gpsimd.dma_start(out=bp1[:, :], in_=b_proj).then_inc(sG2, 16)
                    gpsimd.dma_start(out=bb1[:, :], in_=base_b).then_inc(sG2, 16)
                    gpsimd.dma_start(out=rb_sb[:, :], in_=router_b).then_inc(sG2, 16)

                xo4 = xo_sb.rearrange("p f (j q) -> p f j q", j=NJ)

                @block.sync
                def _(sync):
                    sync.wait_ge(sT, T_p0)
                    sync.wait_ge(sV, V_p0)
                    sync.wait_ge(sC, C_p0)
                    sync.wait_ge(sSTG[0], 64)
                    sync.wait_ge(sSTG[1], 64)
                    for dc in range(CH):
                        if dc >= 4:
                            sync.wait_ge(sT, T_15dc[dc - 4])
                        sync.dma_start(
                            out=xo4[:, dc % 4, :, :],
                            in_=agi_x[:, dc, :, :],
                        ).then_inc(sXO[dc % 4], 16)

                @block.tensor
                def _(tensor):
                    tensor.wait_ge(sG2, G_p15)
                    tensor.wait_ge(sCON, CON0)
                    for dc in range(CH):
                        if dc >= 2:
                            tensor.wait_ge(sV, V_aTe[dc - 2])
                        tensor.transpose(
                            ptrb[dc % 2][:, :], a_raw[:, dc * 128:(dc + 1) * 128],
                            c_idb[:, :]).then_inc(sT, 1)
                    for dc in range(CH):
                        if dc >= 2:
                            tensor.wait_ge(sV, V_rwTe[dc - 2])
                        else:
                            tensor.wait_ge(sV, V_aTe[CH - 2 + dc])
                        tensor.transpose(
                            ptrb[dc % 2][:, 0:15], rw_raw[:, dc * 128:(dc + 1) * 128],
                            c_idb[0:15, 0:15]).then_inc(sT, 1)
                    for dc in range(CH):
                        tensor.wait_ge(sXO[dc % 4], 16 * (dc // 4 + 1))
                        tensor.wait_ge(sV, V_rwTe[dc])
                        for t2 in range(2):
                            tensor.matmul(
                                ph[t2][:, :], aT_sb[:, dc, :],
                                xo_sb[:, dc % 4, t2 * 512:(t2 + 1) * 512],
                                start=(dc == 0), stop=(dc == CH - 1))
                        for t2 in range(2):
                            mm = tensor.matmul(
                                plg[t2][0:15, :], rwT_sb[:, dc, :],
                                xo_sb[:, dc % 4, t2 * 512:(t2 + 1) * 512],
                                start=(dc == 0), stop=(dc == CH - 1))
                            if t2 == 1:
                                mm.then_inc(sT, 1)
                    for tt in range(8):
                        tensor.wait_ge(sV, V_lgev[1])
                        if tt >= 1:
                            tensor.wait_ge(sV, V_lgTe[tt - 1])
                        tensor.transpose(
                            ptrf[:, 0:15], lg_sb[:, tt * 128:(tt + 1) * 128],
                            c_idf[0:15, 0:15]).then_inc(sT, 1)
                    for tt in range(8):
                        tensor.wait_ge(sV, V_sm[tt])
                        if tt >= 1:
                            tensor.wait_ge(sV, V_mpTe[tt - 1])
                        else:
                            tensor.wait_ge(sV, V_lgTe[7])
                        tensor.transpose(
                            ptrf[0:15, 0:128], mp_all[:, tt, :], c_idf[:, :]
                        ).then_inc(sT, 1)
                    for t2 in range(2):
                        tensor.wait_ge(sV, V_mmap)
                        tensor.wait_ge(sV, V_mpTe[7])
                        if t2 == 1:
                            tensor.wait_ge(sV, V_wev[0])
                        tensor.matmul(
                            pw[0:8, :], mmap[:, :], mpT_sb[:, t2 * 512:(t2 + 1) * 512],
                            start=True, stop=True).then_inc(sT, 1)
                    for t2 in range(2):
                        tensor.wait_ge(sV, V_wev[1])
                        if t2 == 1:
                            tensor.wait_ge(sV, V_wbev[0])
                        tensor.matmul(
                            pw[:, :], c_expand[:, :], wt_sb[0:8, t2 * 512:(t2 + 1) * 512],
                            start=True, stop=True).then_inc(sT, 1)

                @block.vector
                def _(vector):
                    vector.wait_ge(sC, C_p0)
                    vector.wait_ge(sSTG[0], 64)
                    vector.wait_ge(sSTG[1], 64)
                    vector.wait_ge(sCON, CON0)
                    for dc in range(CH):
                        vector.wait_ge(sT, T_aT[dc])
                        vector.tensor_copy(aT_sb[:, dc, :], ptrb[dc % 2][:, :]).then_inc(sV, 1)
                    for dc in range(CH):
                        vector.wait_ge(sT, T_rwT[dc])
                        vector.tensor_copy(rwT_sb[:, dc, :], ptrb[dc % 2][:, 0:15]).then_inc(sV, 1)
                    vector.wait_ge(sG2, G_p15)
                    vector.tensor_copy(em_f[:, :], em_i[:, :])
                    vector.drain()
                    vector.tensor_scalar(mm_t[:, :], c_erange[:, :], em_f[:, :], None,
                                         op0=ALU.subtract).then_inc(sV, 1)
                    vector.wait_ge(sC, C_mmap)
                    vector.tensor_scalar(mm_sq2[:, :], mm_sq[:, :], -2.0, 2.0,
                                         op0=ALU.mult, op1=ALU.add)
                    vector.drain()
                    vector.tensor_scalar(mmap[:, :], mm_sq2[:, :], 0.0, None,
                                         op0=ALU.max).then_inc(sV, 1)
                    vector.wait_ge(sG2, G_p15)
                    vector.tensor_tensor(bias_bf[:, :], bp1[:, :], bb1[:, :], op=ALU.add).then_inc(sV, 1)
                    for t2 in range(2):
                        vector.wait_ge(sT, T_15dc[CH - 1])
                        vector.tensor_scalar(
                            lg_sb[:, t2 * 512:(t2 + 1) * 512], plg[t2][0:15, :],
                            rb_sb[:, :], None, op0=ALU.add).then_inc(sV, 1)
                    for tt in range(8):
                        vector.wait_ge(sT, T_lgT[tt])
                        vector.tensor_copy(lgT_sb[:, tt, :], ptrf[:, 0:15]).then_inc(sV, 1)
                    for tt in range(8):
                        vector.wait_ge(sV, V_lgTe[tt])
                        vector.reduce_max(sm_nm[:, :], lgT_sb[:, tt, :], axis=AX.X,
                                          negate=True).then_inc(sV, 1)
                        vector.wait_ge(sC, C_smexp[tt])
                        vector.reduce_sum(sm_s[:, :], sm_e[:, :], axis=AX.X)
                        vector.drain()
                        vector.reciprocal(sm_r[:, :], sm_s[:, :])
                        vector.drain()
                        vector.tensor_scalar(sm_p[:, :], sm_e[:, :], sm_r[:, :], None,
                                             op0=ALU.mult)
                        vector.drain()
                        vector.max(sm_m8[:, :], sm_p[:, :])
                        vector.drain()
                        vector.scalar_tensor_tensor(
                            mp_all[:, tt, :], sm_p[:, :], sm_m8[:, 7:8], sm_p[:, :],
                            op0=ALU.is_ge, op1=ALU.mult).then_inc(sV, 1)
                    for tt in range(8):
                        vector.wait_ge(sT, T_mpT[tt])
                        vector.tensor_copy(mpT_sb[:, tt * 128:(tt + 1) * 128],
                                           ptrf[0:15, 0:128]).then_inc(sV, 1)
                    for t2 in range(2):
                        vector.wait_ge(sT, T_wmm[t2])
                        vector.tensor_copy(wt_sb[0:8, t2 * 512:(t2 + 1) * 512],
                                           pw[0:8, :]).then_inc(sV, 1)
                    for t2 in range(2):
                        vector.wait_ge(sT, T_wbmm[t2])
                        vector.tensor_copy(wb_sb[:, t2 * 512:(t2 + 1) * 512],
                                           pw[:, :]).then_inc(sV, 1)
                    for t2 in range(2):
                        vector.wait_ge(sV, V_wbev[t2])
                        vector.tensor_tensor(
                            g_sb[:, t2 * 512:(t2 + 1) * 512], ph[t2][:, :],
                            wb_sb[:, t2 * 512:(t2 + 1) * 512], op=ALU.mult).then_inc(sV, 1)

                @block.scalar
                def _(scalar):
                    scalar.wait_ge(sV, V_mmt)
                    scalar.activation(mm_sq[:, :], mm_t[:, :], AF.Square).then_inc(sC, 1)
                    for tt in range(8):
                        scalar.wait_ge(sV, V_smn[tt])
                        scalar.activation(sm_e[:, :], lgT_sb[:, tt, :], AF.Exp,
                                          bias=sm_nm[:, :], scale=1.0).then_inc(sC, 1)

            with contextlib.ExitStack() as _st2:
                xt_sb = _st2.enter_context(nc.sbuf_tensor("xt_sb", [128, 2, CH, 512], BF16))
                ev_sb = _st2.enter_context(nc.sbuf_tensor("ev_sb", [128, 12, 512], BF16))
                p1_0 = _st2.enter_context(nc.psum_tensor("p1_0", [128, 512], F32))
                p1_1 = _st2.enter_context(nc.psum_tensor("p1_1", [128, 512], F32))
                p1_2 = _st2.enter_context(nc.psum_tensor("p1_2", [128, 512], F32))
                p1_3 = _st2.enter_context(nc.psum_tensor("p1_3", [128, 512], F32))
                p1_4 = _st2.enter_context(nc.psum_tensor("p1_4", [128, 512], F32))
                p1_5 = _st2.enter_context(nc.psum_tensor("p1_5", [128, 512], F32))
                p1_6 = _st2.enter_context(nc.psum_tensor("p1_6", [128, 512], F32))
                p1_7 = _st2.enter_context(nc.psum_tensor("p1_7", [128, 512], F32))
                banks = [p1_0, p1_1, p1_2, p1_3, p1_4, p1_5, p1_6, p1_7]

                @block.scalar
                def _(scalar):
                    scalar.wait_ge(sC, C_p15)
                    scalar.wait_ge(sV, V_p15)
                    scalar.wait_ge(sT, T_p15)
                    scalar.wait_ge(sSTG[0], 64)
                    scalar.wait_ge(sSTG[1], 64)
                    for i in range(4):
                        scalar.wait_ge(sXO[i], 16 * 8)
                    xt4 = xt_sb.rearrange("p t c (h q) -> p t c h q", h=2)
                    for tc in range(NTC):
                        h, s = tc // 8, tc % 8
                        p = tc % 2
                        scalar.wait_ge(sCC, 2 * (h + 1))
                        if tc >= 2:
                            scalar.wait_ge(sT, T_p1[(tc - 2, 11)])
                        for q2 in range(2):
                            for c2 in range(2):
                                scalar.dma_start(
                                    out=xt4[:, p, c2 * 16:(c2 + 1) * 16, q2, :],
                                    in_=ago_v[2 * h + q2, s, c2 * 16:(c2 + 1) * 16, :, :]
                                    .rearrange("c p q -> p c q"),
                                ).then_inc(sX1[p], 16)

                @block.tensor
                def _(tensor):
                    tensor.wait_ge(sG, G_bw)
                    tensor.wait_ge(sONE, 1)
                    tensor.wait_ge(sV, V_p15)
                    for tc in range(NTC):
                        p = tc % 2
                        tensor.wait_ge(sX1[p], 64 * (tc // 2 + 1))
                        for jj in range(8):
                            if tc >= 1:
                                pk = 8 + jj if jj < 4 else jj
                                tensor.wait_ge(sV, V_p1[(tc - 1, pk)])
                            for dc in range(CH):
                                mm = tensor.matmul(
                                    banks[jj][:, :], wqk_sb[:, dc, jj * 128:(jj + 1) * 128],
                                    xt_sb[:, p, dc, :],
                                    start=(dc == 0), stop=(dc == CH - 1))
                                if dc == CH - 1:
                                    mm.then_inc(sT, 1)
                        for t4 in range(4):
                            tensor.wait_ge(sV, V_p1[(tc, t4)])
                            for dc in range(CH):
                                tensor.matmul(
                                    banks[t4][:, :], xt_sb[:, p, dc, t4 * 128:(t4 + 1) * 128],
                                    wv_sb[:, dc, :], start=(dc == 0), stop=False)
                            tensor.matmul(
                                banks[t4][:, :], ones_bf_r[:, :], bv_bf[:, :],
                                start=False, stop=True).then_inc(sT, 1)

                @block.vector
                def _(vector):
                    vector.wait_ge(sCON, CON0)
                    vector.wait_ge(sCONA, CONA)
                    for tc in range(NTC):
                        if tc >= 1:
                            vector.wait_ge(
                                sQK[(tc - 1) % 2], 192 * ((tc - 1) // 2 + 1))
                        for k in range(12):
                            vector.wait_ge(sT, T_p1[(tc, k)])
                            if k < 8:
                                vector.tensor_scalar(
                                    ev_sb[:, k, :], banks[k][:, :], bqk_sb[:, k:k + 1],
                                    None, op0=ALU.add).then_inc(sV, 1)
                            else:
                                vector.tensor_copy(
                                    ev_sb[:, k, :], banks[k - 8][:, :]).then_inc(sV, 1)

                @block.sync
                def _(sync):
                    for tc in range(NTC):
                        h, s = tc // 8, tc % 8
                        p = tc % 2
                        col = s * 1024 + h * 512
                        vrow = s * 8 + h * 4
                        for k in range(12):
                            sync.wait_ge(sV, V_p1[(tc, k)])
                            if k < 8:
                                dst = qT_loc if k < 4 else kT_loc
                                sync.dma_start(
                                    out=dst[k % 4, :, col:col + 512],
                                    in_=ev_sb[:, k, :],
                                ).then_inc(sQK[p], 16)
                            else:
                                sync.dma_start(
                                    out=v_loc[vrow + (k - 8), :, :],
                                    in_=ev_sb[:, k, :],
                                ).then_inc(sQK[p], 16)

        with contextlib.ExitStack() as _st2:
            c_et = _st2.enter_context(nc.sbuf_tensor("c_et", [128, 20, 512], BF16))
            q_sb = _st2.enter_context(nc.sbuf_tensor("q_sb", [128, 3, 4096], BF16))
            k_sb = _st2.enter_context(nc.sbuf_tensor("k_sb", [128, 3, 4096], BF16))
            v_sb = _st2.enter_context(nc.sbuf_tensor("v_sb", [128, 3, CH, 128], BF16))
            pe_sb = _st2.enter_context(nc.sbuf_tensor("pe_sb", [128, 4, 512], BF16))
            p_sb = _st2.enter_context(nc.sbuf_tensor("p_sb", [128, 4, 512], BF16))
            bc_sb = _st2.enter_context(nc.sbuf_tensor("bc_sb", [128, 2, 512], F32))
            onrm_sb = _st2.enter_context(nc.sbuf_tensor("onrm_sb", [128, 2, 512], BF16))
            wcv = _st2.enter_context(nc.sbuf_tensor("wcv", [128, 3, 2, D], BF16))
            ps_s0 = _st2.enter_context(nc.psum_tensor("ps_s0", [128, 512], F32))
            ps_s1 = _st2.enter_context(nc.psum_tensor("ps_s1", [128, 512], F32))
            ps_s2 = _st2.enter_context(nc.psum_tensor("ps_s2", [128, 512], F32))
            ps_s3 = _st2.enter_context(nc.psum_tensor("ps_s3", [128, 512], F32))
            ps_o0 = _st2.enter_context(nc.psum_tensor("ps_o0", [128, 512], F32))
            ps_o1 = _st2.enter_context(nc.psum_tensor("ps_o1", [128, 512], F32))
            ps_dn0 = _st2.enter_context(nc.psum_tensor("ps_dn0", [128, 512], F32))
            ps_dn1 = _st2.enter_context(nc.psum_tensor("ps_dn1", [128, 512], F32))
            ps_s = [ps_s0, ps_s1, ps_s2, ps_s3]
            ps_o = [ps_o0, ps_o1]
            ps_den = [ps_dn0, ps_dn1]

            def _p2_prefetch(sync, u):
                b, hh = u // 4, u % 4
                pu = u % 3
                sync.dma_start(
                    out=q_sb[:, pu, :], in_=qT_loc[hh, :, b * 4096:(b + 1) * 4096]
                ).then_inc(sU[pu], 16)
                sync.dma_start(
                    out=k_sb[:, pu, :], in_=kT_loc[hh, :, b * 4096:(b + 1) * 4096]
                ).then_inc(sU[pu], 16)
                for kq in range(4):
                    sync.dma_start(
                        out=v_sb[:, pu, kq * 8:(kq + 1) * 8, :],
                        in_=v_loc[b * 32 + kq * 8:b * 32 + (kq + 1) * 8, :,
                                  hh * 128:(hh + 1) * 128].rearrange("t p d -> p t d"),
                    ).then_inc(sU[pu], 16)

            @block.sync
            def _(sync):
                sync.wait_ge(sT, T_p1b)
                sync.wait_ge(sQK[0], QK_p1b)
                sync.wait_ge(sQK[1], QK_p1b)
                sync.wait_ge(sV, V_p1b)
                sync.wait_ge(sC, C_p15)
                sync.dma_start(out=c_et[:, :, :], in_=et_in).then_inc(sCON, 16)
                for u in range(3):
                    _p2_prefetch(sync, u)
                for u in range(NU):
                    b, hh = u // 4, u % 4
                    for qc in range(NQC):
                        g = u * NQC + qc
                        sync.wait_ge(sV, V_onrm[g])
                        shard = (b * 4096 + qc * 512) // 1024
                        sync.dma_start(
                            out=a2a_in_v[shard, hh * 128:(hh + 1) * 128,
                                         (qc % 2) * 512:(qc % 2) * 512 + 512],
                            in_=onrm_sb[:, g % 2, :],
                        ).then_inc(sA[g % 2], 16)
                    if u + 3 < NU:
                        sync.wait_ge(sT, T_pvf[(u) * NQC + NQC - 1])
                        _p2_prefetch(sync, u + 3)

            @block.tensor
            def _(tensor):
                for u in range(NU):
                    pu = u % 3
                    tensor.wait_ge(sU[pu], U_val[u])
                    for i in range(NPAIR + LAG):
                        if i < NPAIR:
                            gp = u * NPAIR + i
                            qc, kt = PAIRS[i][0], PAIRS[i][1]
                            if gp >= 4:
                                tensor.wait_ge(sC, C_exp[gp - 4])
                            tensor.matmul(
                                ps_s[gp % 4][:, :], k_sb[:, pu, kt * 128:(kt + 1) * 128],
                                q_sb[:, pu, qc * 512:(qc + 1) * 512],
                                start=True, stop=True).then_inc(sT, 1)
                        j = i - LAG
                        if j >= 0:
                            gj = u * NPAIR + j
                            qc, kt, nk, first, last = PAIRS[j]
                            g = u * NQC + qc
                            tensor.wait_ge(sV, V_stt[gj])
                            if first and g >= 2:
                                tensor.wait_ge(sV, V_rcp[g - 2])
                            mm = tensor.matmul(
                                ps_den[g % 2][:, :], ones_bf_sq[:, :], p_sb[:, gj % 4, :],
                                start=first, stop=last)
                            if last:
                                mm.then_inc(sT, 1)
                            if first and g >= 2:
                                tensor.wait_ge(sV, V_onrm[g - 2])
                            mm = tensor.matmul(
                                ps_o[g % 2][:, :], v_sb[:, pu, kt, :], p_sb[:, gj % 4, :],
                                start=first, stop=last)
                            if last:
                                mm.then_inc(sT, 1)

            @block.vector
            def _(vector):
                vector.wait_ge(sCON, CON1)
                for u in range(NU):
                    hh = u % 4
                    pend = []
                    for i in range(NPAIR + LAG + 2):
                        if i < NPAIR:
                            gp = u * NPAIR + i
                            qc, kt = PAIRS[i][0], PAIRS[i][1]
                            m = kt - 4 * qc
                            esel = hh * 5 + (0 if m < 0 else 1 + m)
                            vector.wait_ge(sC, C_exp[gp])
                            if gp >= 4:
                                vector.wait_ge(sT, T_S[gp])
                            vector.tensor_tensor(
                                p_sb[:, gp % 4, :], pe_sb[:, gp % 4, :],
                                c_et[:, esel, :], op=ALU.mult
                            ).then_inc(sV, 1)
                        j = i - LAG
                        if 0 <= j < NPAIR and PAIRS[j][4]:
                            pend.append((u * NQC + PAIRS[j][0], i + 2))
                        while pend and pend[0][1] <= i:
                            g = pend.pop(0)[0]
                            vector.wait_ge(sT, T_denf[g])
                            vector.reciprocal(
                                bc_sb[:, g % 2, :], ps_den[g % 2][:, :]).then_inc(sV, 1)
                            vector.drain()
                            vector.wait_ge(sT, T_pvf[g])
                            if g >= 2:
                                vector.wait_ge(sA[g % 2], 16 * (g // 2))
                            vector.tensor_tensor(
                                onrm_sb[:, g % 2, :], ps_o[g % 2][:, :], bc_sb[:, g % 2, :],
                                op=ALU.mult).then_inc(sV, 1)

            @block.scalar
            def _(scalar):
                for u in range(NU):
                    hh = u % 4
                    for i in range(NPAIR):
                        gp = u * NPAIR + i
                        qc, kt = PAIRS[i][0], PAIRS[i][1]
                        m = kt - 4 * qc
                        bidx = hh * 32 + (m + 28)
                        scalar.wait_ge(sT, T_S[gp])
                        if gp >= 4:
                            scalar.wait_ge(sV, V_stt[gp - 4])
                        scalar.activation(
                            pe_sb[:, gp % 4, :], ps_s[gp % 4][:, :], AF.Exp,
                            bias=c_bias[:, bidx:bidx + 1], scale=1.0 / 128.0
                        ).then_inc(sC, 1)

            @block.gpsimd
            def _(gpsimd):
                gpsimd.wait_ge(sT, T_p1b)
                gpsimd.wait_ge(sV, V_p1b)
                gpsimd.wait_ge(sC, C_p15)
                gpsimd.wait_ge(sQK[0], QK_p1b)
                gpsimd.wait_ge(sQK[1], QK_p1b)
                convs = []
                for (srcw, dstw) in ((base_WT, bwb), (w_proj, wpb)):
                    sv = srcw.rearrange("(c p) n -> p c n", p=128)
                    dv = dstw.rearrange("(c p) n -> p c n", p=128)
                    for ch in range(16):
                        convs.append((sv, dv, ch))
                for idx in range(len(convs) + 2):
                    if idx < len(convs):
                        sv, dv, ch = convs[idx]
                        sl = idx % 3
                        if idx >= 3:
                            gpsimd.wait_ge(sWCs[sl], 16 * ((idx - 3) // 3 + 1))
                        gpsimd.dma_start(
                            out=wcv[:, sl, :, :], in_=sv[:, 2 * ch:2 * ch + 2, :],
                        ).then_inc(sWCl[sl], 16)
                    jdx = idx - 2
                    if jdx >= 0:
                        sv, dv, ch = convs[jdx]
                        sl2 = jdx % 3
                        gpsimd.wait_ge(sWCl[sl2], 16 * (jdx // 3 + 1))
                        gpsimd.dma_start(
                            out=dv[:, 2 * ch:2 * ch + 2, :], in_=wcv[:, sl2, :, :],
                        ).then_inc(sWCs[sl2], 16)
                gpsimd.wait_ge(sA[0], 16 * 32)
                gpsimd.wait_ge(sA[1], 16 * 32)
                with allow_ici_all_to_all():
                    gpsimd.collective_compute(
                        "AllToAll", ALU.bypass, replica_groups=groups,
                        ins=[a2a_in[:, :]], outs=[a2a_out[:, :]],
                    ).then_inc(sCC, 1)

        with contextlib.ExitStack() as _st2:
            ao_sb = _st2.enter_context(nc.sbuf_tensor("ao_sb", [128, CH, 1024], BF16))
            wp_sb = _st2.enter_context(nc.sbuf_tensor("wp_sb", [128, 2, 16, 512], BF16))
            bwv_sb = _st2.enter_context(nc.sbuf_tensor("bwv_sb", [128, 2, 16, 512], BF16))
            x3_sb = _st2.enter_context(nc.sbuf_tensor("x3_sb", [128, 2, 4, 512], F32))
            xo3_sb = _st2.enter_context(nc.sbuf_tensor("xo3_sb", [128, 4, 1024], BF16))
            xo34 = xo3_sb.rearrange("p f (j q) -> p f j q", j=NJ)
            accs = _st2.enter_context(nc.sbuf_tensor("accs", [128, 2, 8, 512], BF16))
            out_sb = _st2.enter_context(nc.sbuf_tensor("out_sb", [128, 2, 512], F32))
            p3_0 = _st2.enter_context(nc.psum_tensor("p3_0", [128, 512], F32))
            p3_1 = _st2.enter_context(nc.psum_tensor("p3_1", [128, 512], F32))
            p3_2 = _st2.enter_context(nc.psum_tensor("p3_2", [128, 512], F32))
            p3_3 = _st2.enter_context(nc.psum_tensor("p3_3", [128, 512], F32))
            p3_4 = _st2.enter_context(nc.psum_tensor("p3_4", [128, 512], F32))
            p3_5 = _st2.enter_context(nc.psum_tensor("p3_5", [128, 512], F32))
            p3_6 = _st2.enter_context(nc.psum_tensor("p3_6", [128, 512], F32))
            p3_7 = _st2.enter_context(nc.psum_tensor("p3_7", [128, 512], F32))
            b3 = [p3_0, p3_1, p3_2, p3_3, p3_4, p3_5, p3_6, p3_7]

            @block.sync
            def _(sync):
                sync.wait_ge(sT, T_p2)
                sync.wait_ge(sV, V_p2)
                sync.wait_ge(sC, C_p2)
                sync.wait_ge(sWCs[0], 176)
                sync.wait_ge(sWCs[1], 176)
                sync.wait_ge(sWCs[2], 160)
                for oc in range(2):
                    sync.wait_ge(sV, V_a[(oc, 7)])
                    sync.dma_start(
                        out=acc_d[oc, :, :, :], in_=accs[:, oc % 2, :, :],
                    ).then_inc(sAC[oc % 2], 16)
                sync.wait_ge(sCC, NJ + 1)
                for cc in range(CH):
                    sync.dma_start(
                        out=ao_sb[:, cc, :],
                        in_=a2a_out_v[cc // 4, (cc % 4) * 128:(cc % 4 + 1) * 128, :],
                    ).then_inc(sAO, 16)
                for oc in range(2, 8):
                    sync.wait_ge(sV, V_a[(oc, 7)])
                    sync.dma_start(
                        out=acc_d[oc, :, :, :], in_=accs[:, oc % 2, :, :],
                    ).then_inc(sAC[oc % 2], 16)
                for oc in range(2):
                    sync.wait_ge(sAC[oc % 2], 64)
                    sync.dma_start(
                        out=accs[:, oc % 2, :, :], in_=acc_d[oc, :, :, :],
                    ).then_inc(sACL[oc % 2], 16)
                for oc in range(8):
                    for t8 in range(8):
                        idx = oc * 8 + t8
                        sync.wait_ge(sV, V_b[(oc, t8)])
                        sync.dma_start(
                            out=out_own[t8 * 128:(t8 + 1) * 128, oc * 512:(oc + 1) * 512],
                            in_=out_sb[:, t8 % 2, :],
                        ).then_inc(sOUT[idx % 2], 16)
                    if oc + 2 < 8:
                        sync.dma_start(
                            out=accs[:, oc % 2, :, :], in_=acc_d[oc + 2, :, :, :],
                        ).then_inc(sACL[oc % 2], 16)
                sync.wait_ge(sOUT[0], 16 * 32)
                sync.wait_ge(sOUT[1], 16 * 32)

            @block.gpsimd
            def _(gpsimd):
                gpsimd.wait_ge(sT, T_p2)
                gpsimd.wait_ge(sV, V_p2)
                gpsimd.wait_ge(sWCs[0], 176)
                gpsimd.wait_ge(sWCs[1], 176)
                gpsimd.wait_ge(sWCs[2], 160)
                for oc in range(8):
                    for half in range(2):
                        if oc >= 1:
                            gpsimd.wait_ge(sT, T_dc3[(oc - 1) * CH + half * 16 + 15])
                        gpsimd.dma_start(
                            out=bwv_sb[:, half, :, :],
                            in_=bwb.rearrange("(c p) o -> p c o", p=128)[
                                :, half * 16:(half + 1) * 16, oc * 512:(oc + 1) * 512],
                        ).then_inc(sW0, 16)
                for oc in range(8):
                    for half in range(2):
                        if oc >= 1:
                            gpsimd.wait_ge(
                                sT, T_cc3[(oc - 1) * CH + 15] if half == 0
                                else T_b3[(oc - 1, 7)])
                        gpsimd.dma_start(
                            out=wp_sb[:, half, :, :],
                            in_=wpb.rearrange("(c p) n -> p c n", p=128)[
                                :, half * 16:(half + 1) * 16, oc * 512:(oc + 1) * 512],
                        ).then_inc(sW1, 16)

            @block.scalar
            def _(scalar):
                scalar.wait_ge(sT, T_p2)
                scalar.wait_ge(sV, V_p2)
                scalar.wait_ge(sWCs[0], 176)
                scalar.wait_ge(sWCs[1], 176)
                scalar.wait_ge(sWCs[2], 160)
                for oc in range(8):
                    for dc in range(CH):
                        gdc = oc * CH + dc
                        if gdc >= 4:
                            scalar.wait_ge(sT, T_dc3[gdc - 4])
                        scalar.dma_start(
                            out=xo34[:, dc % 4, :, :],
                            in_=agi_x[:, dc, :, :],
                        ).then_inc(sXO3[dc % 4], 16)
                for oc in range(8):
                    for half in range(2):
                        if oc >= 1:
                            scalar.wait_ge(sV, V_b[(oc - 1, half * 4 + 3)])
                        scalar.dma_start(
                            out=x3_sb[:, half, :, :],
                            in_=x_own.rearrange("(t p) n -> p t n", p=128)[
                                :, half * 4:(half + 1) * 4, oc * 512:(oc + 1) * 512],
                        ).then_inc(sX3[half], 16)

            @block.tensor
            def _(tensor):
                tensor.wait_ge(sV, V_p15)
                for oc in range(8):
                    tensor.wait_ge(sW0, 32 * (oc + 1))
                    for dc in range(CH):
                        gdc = oc * CH + dc
                        tensor.wait_ge(sXO3[dc % 4], 16 * (gdc // 4 + 1))
                        for t8 in range(8):
                            if dc == 0 and oc >= 1:
                                tensor.wait_ge(sV, V_a[(oc - 1, t8)])
                            mm = tensor.matmul(
                                b3[t8][:, :], xo3_sb[:, dc % 4, t8 * 128:(t8 + 1) * 128],
                                bwv_sb[:, dc // 16, dc % 16, :], start=(dc == 0), stop=False)
                            if t8 == 7:
                                mm.then_inc(sT, 1)
                    for t8 in range(8):
                        tensor.matmul(
                            b3[t8][:, :], g_sb[:, t8 * 128:(t8 + 1) * 128],
                            bp_sb[:, oc * 512:(oc + 1) * 512], start=False, stop=False)
                    for t8 in range(8):
                        tensor.matmul(
                            b3[t8][:, :], ones_bf_r[:, :],
                            bias_bf[:, oc * 512:(oc + 1) * 512], start=False,
                            stop=True).then_inc(sT, 1)
                tensor.wait_ge(sAO, 16 * 32)
                for oc in range(8):
                    tensor.wait_ge(sW1, 32 * (oc + 1))
                    for cc in range(CH):
                        for t8 in range(8):
                            if cc == 0:
                                tensor.wait_ge(
                                    sV, V_a[(7, t8)] if oc == 0 else V_b[(oc - 1, t8)])
                            mm = tensor.matmul(
                                b3[t8][:, :], ao_sb[:, cc, t8 * 128:(t8 + 1) * 128],
                                wp_sb[:, cc // 16, cc % 16, :],
                                start=(cc == 0), stop=(cc == CH - 1))
                            if t8 == 7 and cc < CH - 1:
                                mm.then_inc(sT, 1)
                            if cc == CH - 1:
                                mm.then_inc(sT, 1)

            @block.vector
            def _(vector):
                for oc in range(8):
                    for t8 in range(8):
                        vector.wait_ge(sT, T_a3[(oc, t8)])
                        if oc >= 2 and t8 == 0:
                            vector.wait_ge(sAC[oc % 2], 16 * ((oc - 2) // 2 + 1))
                        vector.tensor_copy(
                            accs[:, oc % 2, t8, :], b3[t8][:, :]).then_inc(sV, 1)
                for oc in range(8):
                    for t8 in range(8):
                        idx = oc * 8 + t8
                        vector.wait_ge(sT, T_b3[(oc, t8)])
                        vector.wait_ge(sX3[t8 // 4], 16 * (oc + 1))
                        vector.wait_ge(sACL[oc % 2], 16 * (oc // 2 + 1))
                        if idx >= 2:
                            vector.wait_ge(sOUT[idx % 2], 16 * (idx // 2))
                        vector.tensor_tensor(
                            out_sb[:, t8 % 2, :], b3[t8][:, :], x3_sb[:, t8 // 4, t8 % 4, :],
                            op=ALU.add)
                        vector.drain()
                        vector.tensor_tensor(
                            out_sb[:, t8 % 2, :], out_sb[:, t8 % 2, :],
                            accs[:, oc % 2, t8, :], op=ALU.add).then_inc(sV, 1)

    return nc


_NC_CACHE = {}


def _get_program():
    if "nc" not in _NC_CACHE:
        _NC_CACHE["nc"] = build_program()
    return _NC_CACHE["nc"]


def _e_tiles(sl):
    ki = np.arange(128, dtype=np.float64)[:, None]
    qi = np.arange(512, dtype=np.float64)[None, :]
    out = np.zeros((128, 20, 512), dtype=np.float32)
    for hh in range(HPC):
        out[:, hh * 5, :] = np.exp(sl[hh] * (ki - qi - 128.0))
        for mm in range(4):
            out[:, hh * 5 + 1 + mm, :] = (
                np.exp(sl[hh] * (ki - qi + 128 * mm)) * ((ki + 128 * mm) <= qi))
    return out.astype(ml_dtypes.bfloat16)


def prep_in_maps(L0_attn_hidden_states, L0_attn_w_attn, L0_attn_b_attn,
                 L0_attn_w_proj, L0_attn_b_proj, L0_ffn_base_W, L0_ffn_base_b,
                 L0_ffn_router_W, L0_ffn_router_b, L0_ffn_A, L0_ffn_B,
                 L0_ffn_expert_map):
    x = np.ascontiguousarray(L0_attn_hidden_states.reshape(NT, D))
    w_attn = np.asarray(L0_attn_w_attn)
    b_attn = np.asarray(L0_attn_b_attn)
    slopes = _alibi_slopes(32)
    idb = np.eye(128, dtype=ml_dtypes.bfloat16)
    idf = np.eye(128, dtype=np.float32)
    er = np.broadcast_to(np.arange(8, dtype=np.float32)[None, :], (15, 8)).copy()
    ex = np.zeros((8, 128), dtype=np.float32)
    for e in range(8):
        ex[e, e * 16:(e + 1) * 16] = 1.0

    base_WT = np.ascontiguousarray(np.asarray(L0_ffn_base_W).T)
    Bt = np.ascontiguousarray(
        np.asarray(L0_ffn_B).transpose(0, 2, 1).reshape(128, D))
    in_maps = []
    for c in range(NCORE):
        qcols = slice(4 * c * 128, 4 * c * 128 + 512)
        kcols = slice(4096 + 4 * c * 128, 4096 + 4 * c * 128 + 512)
        vcols = slice(8192 + 4 * c * 128, 8192 + 4 * c * 128 + 512)
        wqk_c = np.ascontiguousarray(
            np.concatenate([w_attn[:, qcols], w_attn[:, kcols]], axis=1))
        wv_c = np.ascontiguousarray(w_attn[:, vcols])
        bqk_c = np.ascontiguousarray(
            np.concatenate([b_attn[qcols], b_attn[kcols]]).reshape(8, 128).T)
        bv_c = np.ascontiguousarray(b_attn[vcols].reshape(1, 512))
        sl = slopes[4 * c:4 * c + 4]
        slope128 = np.broadcast_to(128.0 * sl[None, :], (128, HPC)).copy()
        btab = np.zeros((128, HPC * 32), dtype=np.float32)
        for hh in range(HPC):
            for mi in range(28):
                btab[:, hh * 32 + mi] = sl[hh] * 128.0 * (mi - 27)
        in_maps.append({
            "x_own": np.ascontiguousarray(x[c * TOK:(c + 1) * TOK]),
            "wqk": wqk_c, "wv_in": wv_c, "bqk_in": bqk_c, "bv_in": bv_c,
            "w_proj": np.asarray(L0_attn_w_proj),
            "b_proj": np.asarray(L0_attn_b_proj).reshape(1, D),
            "base_WT": base_WT,
            "base_b": np.asarray(L0_ffn_base_b).reshape(1, D),
            "router_W": np.asarray(L0_ffn_router_W),
            "router_b": np.asarray(L0_ffn_router_b).reshape(15, 1),
            "A_in": np.ascontiguousarray(np.asarray(L0_ffn_A).reshape(128, D)),
            "B_in": Bt,
            "emap": np.asarray(L0_ffn_expert_map).reshape(15, 1).astype(np.int32),
            "et_in": _e_tiles(sl), "slope_in": slope128, "btab_in": btab,
            "idb_in": idb, "idf_in": idf, "er_in": er, "ex_in": ex,
        })
    return in_maps


def kernel(**inputs):
    in_maps = prep_in_maps(**inputs)
    nc = _get_program()
    res = run_bass_kernel_spmd(nc, in_maps, core_ids=list(range(NCORE)))
    out = np.empty((NT, D), dtype=np.float32)
    for c in range(NCORE):
        out[c * TOK:(c + 1) * TOK] = res.results[c]["out_own"]
    return out.reshape(2, 4096, 4096)
